# version 04
# baseline (speedup 1.0000x reference)
import contextlib

import numpy as np
import ml_dtypes

import concourse.bass as bass
import concourse.mybir as mybir
from concourse.bass_utils import run_bass_kernel_spmd

F32 = mybir.dt.float32
BF16 = mybir.dt.bfloat16
AF = mybir.ActivationFunctionType
ALU = mybir.AluOpType
AX = mybir.AxisListType

BF = ml_dtypes.bfloat16

B, S, D = 2, 8192, 1024
NCORES = 8
RPC = (B * S) // NCORES
NT = RPC // 128
KD = D // 128
H, DH = 16, 64
HPC = 4
EPS = 1e-5

SLOPES = [2.0 ** (-(h + 1) * 8.0 / H) for h in range(H)]

ENGS = ["sync", "vector", "scalar", "tensor", "gpsimd"]


def f32(x):
    return np.ascontiguousarray(x, dtype=np.float32)


def bf16c(v):
    return float(np.float32(BF(np.float32(v))))


def hi_lo(v):
    hi = bf16c(v)
    lo = bf16c(np.float32(v) - np.float32(hi))
    return hi, lo


class Prog:
    def __init__(self):
        self.nc = bass.Bass("TRN2", target_bir_lowering=False)
        self.es = contextlib.ExitStack()
        self.streams = {e: [] for e in ENGS}
        self.cnt = {}

    def dram_in(self, name, shape, dt=F32):
        return self.nc.dram_tensor(name, list(shape), dt, kind="ExternalInput").ap()

    def dram_out(self, name, shape, dt=F32):
        return self.nc.dram_tensor(name, list(shape), dt, kind="ExternalOutput").ap()

    def sbuf(self, name, shape, dt=F32):
        return self.es.enter_context(self.nc.sbuf_tensor(name, list(shape), dt))

    def psum(self, name, shape, dt=F32):
        return self.es.enter_context(self.nc.psum_tensor(name, list(shape), dt))

    def sem(self, name):
        h = self.es.enter_context(self.nc.semaphore(name))
        self.cnt[h] = 0
        return h

    def op(self, eng, fn, wait=(), inc=None, inc_n=1):
        tgt = None
        if inc is not None:
            self.cnt[inc] += inc_n
            tgt = self.cnt[inc]
        self.streams[eng].append((fn, [(s, v) for (s, v) in wait], inc, inc_n))
        return tgt

    def dma(self, eng, out, in_, wait=(), inc=None, **kw):
        return self.op(eng, lambda e: e.dma_start(out=out, in_=in_, **kw),
                       wait=wait, inc=inc, inc_n=16)

    def val(self, sem):
        return self.cnt[sem]

    def build(self):
        final = [(s, v) for s, v in self.cnt.items() if v > 0]
        with self.nc.Block() as block:
            def runner(ename):
                def f(e):
                    for fn, waits, inc, inc_n in self.streams[ename]:
                        for (s, v) in waits:
                            e.wait_ge(s, v)
                        instr = fn(e)
                        if inc is not None:
                            instr.then_inc(inc, inc_n)
                    for (s, v) in final:
                        e.wait_ge(s, v)
                return f
            block.sync(runner("sync"))
            block.vector(runner("vector"))
            block.scalar(runner("scalar"))
            block.tensor(runner("tensor"))
            block.gpsimd(runner("gpsimd"))
        self.es.close()
        return self.nc



class LNBufs:
    def __init__(self, p, tag, nt=NT, d=D):
        self.msum = p.sbuf(f"ln_m_{tag}", [128, nt])
        self.mean = p.sbuf(f"ln_mu_{tag}", [128, nt])
        self.sqs = p.sbuf(f"ln_q_{tag}", [128, nt])
        self.m2 = p.sbuf(f"ln_m2_{tag}", [128, nt])
        self.vln = p.sbuf(f"ln_vl_{tag}", [128, nt])
        self.vr = p.sbuf(f"ln_vr_{tag}", [128, nt])
        self.rstd = p.sbuf(f"ln_r_{tag}", [128, nt])
        self.epsb = p.sbuf(f"ln_e_{tag}", [128, 1])
        self.sc = p.sbuf(f"ln_s_{tag}", [128, d])
        self.sq = p.sem(f"ln_sq_{tag}")
        self.xm = p.sem(f"ln_xm_{tag}")
        self.eps_done = None


def _ack0(p, bufs):
    p.op("scalar", lambda e: e.nop(), wait=[(bufs.xm, p.val(bufs.xm))],
         inc=bufs.sq)
    return [(bufs.sq, p.val(bufs.sq))]


def emit_ln(p, bufs, x_of, out_of, nt, wait_in, done, d=D, gamma=None,
            beta=None):
    sq, xmv = bufs.sq, bufs.xm
    inv_d = 1.0 / d
    if bufs.eps_done is None:
        p.op("vector", lambda e: e.memset(bufs.epsb[:], EPS))
        bufs.eps_done = True
    for i in range(nt):
        p.op("vector", lambda e, i=i: e.reduce_sum(
            bufs.msum[:, i:i + 1], x_of(i), AX.X), wait=wait_in(i),
            inc=(xmv if i == nt - 1 else None))
        p.op("scalar", lambda e, i=i: e.activation(
            bufs.sc[:], x_of(i), AF.Square, accum_out=bufs.sqs[:, i:i + 1]),
            wait=wait_in(i), inc=(sq if i == nt - 1 else None))
    def _ack():
        p.op("scalar", lambda e: e.nop(), wait=[(xmv, p.val(xmv))], inc=sq)
        return [(sq, p.val(sq))]
    w0 = _ack0(p, bufs)
    p.op("vector", lambda e: e.tensor_scalar_mul(
        bufs.mean[:, 0:nt], bufs.msum[:, 0:nt], inv_d), wait=w0, inc=xmv)
    w1 = _ack()
    p.op("vector", lambda e: e.tensor_mul(
        bufs.m2[:, 0:nt], bufs.mean[:, 0:nt], bufs.mean[:, 0:nt]),
        wait=w1, inc=xmv)
    w2 = _ack()
    p.op("vector", lambda e: e.scalar_tensor_tensor(
        bufs.vr[:, 0:nt], bufs.sqs[:, 0:nt], inv_d, bufs.m2[:, 0:nt],
        ALU.mult, ALU.subtract), wait=w2 + [(sq, p.val(sq))], inc=xmv)
    p.op("scalar", lambda e: e.activation(
        bufs.vln[:, 0:nt], bufs.vr[:, 0:nt], AF.Ln, bias=bufs.epsb[:]),
        wait=[(xmv, p.val(xmv))], inc=sq)
    p.op("vector", lambda e: e.engine_nop(), wait=[(sq, p.val(sq))], inc=xmv)
    p.op("scalar", lambda e: e.activation(
        bufs.rstd[:, 0:nt], bufs.vln[:, 0:nt], AF.Exp, scale=-0.5),
        wait=[(xmv, p.val(xmv))], inc=sq)
    rv = p.val(sq)
    for i in range(nt):
        p.op("vector", lambda e, i=i: e.tensor_scalar(
            out_of(i), x_of(i), bufs.mean[:, i:i + 1], bufs.rstd[:, i:i + 1],
            ALU.subtract, ALU.mult), wait=[(sq, rv)],
            inc=(done if gamma is None else None))
        if gamma is not None:
            p.op("vector", lambda e, i=i: e.tensor_mul(out_of(i), out_of(i),
                                                       gamma))
            p.op("vector", lambda e, i=i: e.tensor_add(out_of(i), out_of(i),
                                                       beta), inc=done)


def emit_transpose(p, src_of, dst_of, nk, nt, ident, banks, bank_sem, done,
                   wait_of, kgrp=4):
    ngrp = 0
    for i in range(nt):
        for k0 in range(0, nk, kgrp):
            bank = banks[ngrp % 2]
            w = list(wait_of(i))
            if ngrp >= 2:
                w.append((done, p.val(done) - 1))
            for j in range(kgrp):
                p.op("tensor", lambda e, i=i, k=k0 + j, j=j, bank=bank:
                     e.transpose(bank[:, j * 128:(j + 1) * 128],
                                 src_of(i, k), ident[:]),
                     wait=(w if j == 0 else ()),
                     inc=(bank_sem if j == kgrp - 1 else None))
            bv = p.val(bank_sem)
            p.op("vector", lambda e, i=i, k0=k0, bank=bank, kgrp=kgrp:
                 e.tensor_copy(
                     out=dst_of(i, k0),
                     in_=bank[:, 0:kgrp * 128].rearrange("p (k t) -> p k t",
                                                         t=128)),
                 wait=[(bank_sem, bv)], inc=done)
            ngrp += 1



def build_launch_A(debug=False):
    p = Prog()
    x_in = p.dram_in("x", [RPC, D])
    wT = {n: p.dram_in(f"w{n}T", [D, D]) for n in "qkv"}
    qpos = p.dram_in("qpos", [1, RPC])
    qfmod = p.dram_in("qfmod", [1, RPC])
    kfmod = p.dram_in("kfmod", [1, RPC])
    identf = p.dram_in("identf", [128, 128])
    o_q = p.dram_out("q_ext", [H * 67, RPC], BF16)
    o_k = p.dram_out("k_ext", [H * 67, RPC], BF16)
    o_v = p.dram_out("v_tok", [RPC, H * 65], BF16)
    if debug:
        o_dbg = p.dram_out("dbg", [128, 3 * NT])

    x_sb = p.sbuf("x_sb", [128, NT * D])
    xn_fm = p.sbuf("xn_fm", [128, KD * RPC], BF16)
    w_sb = p.sbuf("w_sb", [128, KD * D], BF16)
    stage = p.sbuf("stage", [128, KD * RPC], BF16)
    vrow = [p.sbuf(f"vrow{j}", [128, H * 65], BF16) for j in range(2)]
    qext_row = p.sbuf("qext_row", [1, 3 * RPC], BF16)
    kext_row = p.sbuf("kext_row", [1, 3 * RPC], BF16)
    qf_sb = p.sbuf("qf_sb", [1, RPC])
    kf_sb = p.sbuf("kf_sb", [1, RPC])
    identf_sb = p.sbuf("identf_sb", [128, 128])
    identb_sb = p.sbuf("identb_sb", [128, 128], BF16)
    ln = LNBufs(p, "a")
    tpf = [p.psum(f"tpf{j}", [128, 512]) for j in range(2)]
    tpb = [p.psum(f"tpb{j}", [128, 1024], BF16) for j in range(2)]
    pj = [p.psum(f"pj{j}", [128, 512]) for j in range(4)]

    s_in = p.sem("s_in")
    s_ln = p.sem("s_ln")
    s_tb = p.sem("s_tb")
    s_tr = p.sem("s_tr")
    s_w = p.sem("s_w")
    s_pj = p.sem("s_pj")
    s_ev = p.sem("s_ev")
    s_st = p.sem("s_st")
    s_vr = p.sem("s_vr")

    p.dma("sync", x_sb.ap().rearrange("p (t d) -> p t d", d=D),
          x_in.rearrange("(t p) d -> p t d", p=128), inc=s_in)
    p.dma("sync", qf_sb[:], qfmod[:], inc=s_in)
    p.dma("sync", kf_sb[:], kfmod[:], inc=s_in)
    p.dma("sync", identf_sb[:], identf[:], inc=s_in)
    ldv = p.val(s_in)

    p.op("vector", lambda e: e.tensor_copy(out=identb_sb[:], in_=identf_sb[:]),
         wait=[(s_in, ldv)])

    p.op("vector", lambda e: e.tensor_copy(out=kext_row[0:1, 0:RPC],
                                           in_=kf_sb[:]))
    p.op("vector", lambda e: e.tensor_copy(out=kext_row[0:1, RPC:2 * RPC],
                                           in_=kf_sb[:]))
    p.op("vector", lambda e: e.memset(kext_row[0:1, 2 * RPC:3 * RPC], 1.0),
         inc=s_in)
    consts_v = p.val(s_in)

    xt = lambda i: x_sb[:, i * D:(i + 1) * D]
    emit_ln(p, ln, xt, xt, NT, lambda i: [(s_in, ldv)], s_ln)
    if debug:
        p.dma("sync", o_dbg[:, 0:NT], ln.sqs[:, 0:NT],
              wait=[(s_ln, NT)], inc=s_st)
        p.dma("sync", o_dbg[:, NT:2 * NT], ln.vln[:, 0:NT],
              wait=[(s_ln, NT)], inc=s_st)
        p.dma("sync", o_dbg[:, 2 * NT:3 * NT], ln.rstd[:, 0:NT],
              wait=[(s_ln, NT)], inc=s_st)

    def t_src(i, k):
        return x_sb[:, i * D + k * 128:i * D + (k + 1) * 128]

    def t_dst(i, k0):
        v = xn_fm.ap().rearrange("p (k t) -> p k t", t=RPC)
        return v[:, k0:k0 + 4, i * 128:(i + 1) * 128]

    emit_transpose(p, t_src, t_dst, KD, NT, identf_sb, tpf, s_tb, s_tr,
                   lambda i: [(s_ln, i + 1), (s_in, consts_v)])
    trv = p.val(s_tr)

    for wi, n in enumerate("qkv"):
        wwait = [] if wi == 0 else [(s_pj, p.val(s_pj))]
        p.dma("gpsimd", w_sb.ap().rearrange("p (k o) -> p k o", o=D),
              wT[n].rearrange("(k p) o -> p k o", p=128), wait=wwait, inc=s_w)
        wv = p.val(s_w)
        stwait = [] if wi == 0 else [(s_st, p.val(s_st))]
        seq = 0
        for oc in range(KD):
            for tt in range(RPC // 512):
                bank = pj[seq % 4]
                w = [(s_w, wv), (s_tr, trv)] + stwait
                if seq >= 4:
                    w.append((s_ev, p.val(s_ev) - 3))
                for k in range(KD):
                    p.op("tensor", lambda e, oc=oc, tt=tt, k=k, bank=bank:
                         e.matmul(bank[:],
                                  w_sb[:, k * D + oc * 128:k * D + (oc + 1) * 128],
                                  xn_fm[:, k * RPC + tt * 512:k * RPC + (tt + 1) * 512],
                                  start=(k == 0), stop=(k == KD - 1)),
                         wait=(w if k == 0 else ()),
                         inc=(s_pj if k == KD - 1 else None))
                pv = p.val(s_pj)
                p.op("vector", lambda e, oc=oc, tt=tt, bank=bank: e.tensor_copy(
                    out=stage[:, oc * RPC + tt * 512:oc * RPC + (tt + 1) * 512],
                    in_=bank[:]), wait=[(s_pj, pv)], inc=s_ev)
                seq += 1
        evv = p.val(s_ev)

        if n in "qk":
            dst = o_q if n == "q" else o_k
            for h in range(H):
                oc, half = h // 2, h % 2
                p.dma("sync", dst[67 * h:67 * h + 64, :],
                      stage[64 * half:64 * half + 64, oc * RPC:(oc + 1) * RPC],
                      wait=[(s_ev, evv)], inc=s_st)
            if n == "q":
                for h in range(H):
                    shi, slo = hi_lo(SLOPES[h])
                    p.op("vector", lambda e, shi=shi: e.memset(
                        qext_row[0:1, 0:RPC], shi),
                        wait=[(s_st, p.val(s_st))])
                    p.op("vector", lambda e, slo=slo: e.memset(
                        qext_row[0:1, RPC:2 * RPC], slo))
                    p.op("vector", lambda e, h=h: e.tensor_scalar_mul(
                        qext_row[0:1, 2 * RPC:3 * RPC], qf_sb[:],
                        -SLOPES[h]), wait=[(s_in, consts_v)], inc=s_vr)
                    vv = p.val(s_vr)
                    p.dma("sync",
                          o_q[67 * h + 64:67 * h + 67, :].rearrange(
                              "(o r) t -> o r t", o=1),
                          qext_row.ap().rearrange("o (r t) -> o r t", t=RPC),
                          wait=[(s_vr, vv)], inc=s_st)
            else:
                for h in range(H):
                    p.dma("sync",
                          o_k[67 * h + 64:67 * h + 67, :].rearrange(
                              "(o r) t -> o r t", o=1),
                          kext_row.ap().rearrange("o (r t) -> o r t", t=RPC),
                          wait=[(s_in, consts_v)], inc=s_st)
        else:
            p.op("vector", lambda e: e.memset(vrow[0][:], 1.0))
            p.op("vector", lambda e: e.memset(vrow[1][:], 1.0), inc=s_vr)
            bank_last_evac = [0, 0]
            for i in range(NT):
                vb = vrow[i % 2]
                for k0 in (0, 4):
                    bank = tpb[k0 // 4]
                    w0 = [(s_ev, evv), (s_in, consts_v)]
                    if bank_last_evac[k0 // 4] > 0:
                        w0.append((s_vr, bank_last_evac[k0 // 4]))
                    if i >= 2 and k0 == 0:
                        w0.append((s_st, p.val(s_st) - 16))
                    for j in range(4):
                        p.op("tensor", lambda e, i=i, j=j, k0=k0, bank=bank:
                             e.transpose(bank[:, j * 128:(j + 1) * 128],
                                         stage[:, (k0 + j) * RPC + i * 128:
                                               (k0 + j) * RPC + (i + 1) * 128],
                                         identb_sb[:]),
                             wait=(w0 if j == 0 else ()),
                             inc=(s_tb if j == 3 else None))
                    tbv = p.val(s_tb)
                    src3 = bank[:, 0:512].rearrange("p (h c) -> p h c", c=64)
                    dst3 = vb.ap().rearrange("p (h c) -> p h c", c=65)
                    p.op("vector", lambda e, k0=k0, src3=src3, dst3=dst3:
                         e.tensor_copy(out=dst3[:, 2 * k0:2 * k0 + 8, 0:64],
                                       in_=src3[:, 0:8, :]),
                         wait=[(s_tb, tbv)], inc=s_vr)
                    bank_last_evac[k0 // 4] = p.val(s_vr)
                p.dma("sync", o_v[i * 128:(i + 1) * 128, :], vb[:],
                      wait=[(s_vr, p.val(s_vr))], inc=s_st)

    return p



def build_launch_B():
    p = Prog()
    q_in = p.dram_in("q", [HPC * 67, S], BF16)
    k_in = p.dram_in("k", [HPC * 67, S], BF16)
    v_in = p.dram_in("v", [S, HPC * 65], BF16)
    biastab = p.dram_in("biastab", [128, HPC * 64])
    band = p.dram_in("band", [128, 1920], BF16)
    o_ctx = p.dram_out("ctx", [HPC * 64, S], BF16)

    NKT = S // 128
    NQG = S // 1024

    q_sb = [p.sbuf(f"q{j}", [67, S], BF16) for j in range(2)]
    k_sb = [p.sbuf(f"k{j}", [67, S], BF16) for j in range(2)]
    v_sb = [p.sbuf(f"v{j}", [128, NKT * 65], BF16) for j in range(2)]
    band_sb = p.sbuf("band_sb", [128, 1920], BF16)
    bias_sb = p.sbuf("bias_sb", [128, HPC * 64])
    tanh_sb = [p.sbuf(f"tanh{j}", [128, 1024]) for j in range(2)]
    p_sb = [p.sbuf(f"pp{j}", [128, 1024], BF16) for j in range(2)]
    m30 = p.sbuf("m30", [128, 1])
    ones64 = p.sbuf("ones64", [1, 64])
    recip_sb = p.sbuf("recip_sb", [1, 1024])
    rb_sb = p.sbuf("rb_sb", [64, 1024])
    ctx_sb = [p.sbuf(f"ctx_sb{j}", [64, 1024], BF16) for j in range(2)]

    sc_ps = [p.psum(f"scps{j}", [128, 1024]) for j in range(2)]
    ctx_ps = p.psum("ctxps", [65, 1024])
    rb_ps = p.psum("rbps", [64, 1024])

    s_ld = p.sem("s_ld")
    s_mm = p.sem("s_mm")
    s_sc = p.sem("s_sc")
    s_msk = p.sem("s_msk")
    s_pv = p.sem("s_pv")
    s_ctx = p.sem("s_ctx")
    s_aux = p.sem("s_aux")
    s_ack = p.sem("s_ack")
    s_out = p.sem("s_out")

    p.dma("sync", band_sb[:], band[:], inc=s_ld)
    p.dma("sync", bias_sb[:], biastab[:], inc=s_ld)
    cst = p.val(s_ld)
    p.op("vector", lambda e: e.memset(m30[:], -30.0), wait=[(s_ld, cst)])
    p.op("vector", lambda e: e.memset(ones64[:], 1.0), inc=s_aux)
    p.op("scalar", lambda e: e.nop(), wait=[(s_aux, p.val(s_aux))], inc=s_ack)
    cst_ack = p.val(s_ack)

    pv_end_of_head = {}
    ld_done = {}

    def load_head(hh):
        buf = hh % 2
        w = [] if hh < 2 else [(s_pv, pv_end_of_head[hh - 2])]
        p.dma("sync", q_sb[buf][:], q_in[hh * 67:(hh + 1) * 67, :], wait=w,
              inc=s_ld)
        p.dma("sync", k_sb[buf][:], k_in[hh * 67:(hh + 1) * 67, :], inc=s_ld)
        p.dma("sync", v_sb[buf].ap().rearrange("p (k c) -> p k c", c=65),
              v_in.rearrange("(k p) c -> p k c", p=128)[:, :, hh * 65:(hh + 1) * 65],
              inc=s_ld)
        ld_done[hh] = p.val(s_ld)

    load_head(0)
    load_head(1)

    exp_v = []
    pv_v = []
    mask_v = []

    for hh in range(HPC):
        buf = hh % 2
        for qg in range(NQG):
            nk = 8 * (qg + 1)
            pend = None

            def emit_pv(kt, sbuf_i, wpv, nk, buf=buf):
                for half in range(2):
                    p.op("tensor", lambda e, kt=kt, half=half, sbuf_i=sbuf_i,
                         buf=buf, nk=nk:
                         e.matmul(ctx_ps[:, half * 512:(half + 1) * 512],
                                  v_sb[buf][:, kt * 65:(kt + 1) * 65],
                                  p_sb[sbuf_i][:, half * 512:(half + 1) * 512],
                                  start=(kt == 0), stop=(kt == nk - 1)),
                         wait=(wpv if half == 0 else ()),
                         inc=(s_pv if half == 1 else None))
                pv_v.append(p.val(s_pv))

            for kt in range(nk):
                n = len(exp_v)
                sbuf_i = n % 2
                w0 = [(s_ld, ld_done[hh]), (s_ack, cst_ack)]
                if n >= 2:
                    w0.append((s_sc, exp_v[n - 2]))
                for half in range(2):
                    p.op("tensor", lambda e, kt=kt, qg=qg, half=half,
                         sbuf_i=sbuf_i, buf=buf:
                         e.matmul(sc_ps[sbuf_i][:, half * 512:(half + 1) * 512],
                                  k_sb[buf][:, kt * 128:(kt + 1) * 128],
                                  q_sb[buf][:, qg * 1024 + half * 512:
                                            qg * 1024 + (half + 1) * 512],
                                  start=True, stop=True),
                         wait=(w0 if half == 0 else ()),
                         inc=(s_mm if half == 1 else None))
                mmv = p.val(s_mm)
                j = hh * 64 + (8 * qg - kt + 7)
                p.op("scalar", lambda e, sbuf_i=sbuf_i, j=j:
                     e.activation(tanh_sb[sbuf_i][:], sc_ps[sbuf_i][:],
                                  AF.Tanh, scale=1.0 / 240.0,
                                  bias=bias_sb[:, j:j + 1]),
                     wait=[(s_mm, mmv)])
                wexp = []
                if n >= 2:
                    wexp.append((s_pv, pv_v[n - 2]))
                    if mask_v[n - 2] is not None:
                        wexp.append((s_msk, mask_v[n - 2]))
                p.op("scalar", lambda e, sbuf_i=sbuf_i:
                     e.activation(p_sb[sbuf_i][:], tanh_sb[sbuf_i][:],
                                  AF.Exp, scale=30.0, bias=m30[:]),
                     wait=wexp, inc=s_sc)
                exp_v.append(p.val(s_sc))
                masked = kt >= 8 * qg
                if masked:
                    off = 896 + 1024 * qg - 128 * kt
                    p.op("vector", lambda e, sbuf_i=sbuf_i, off=off:
                         e.tensor_mul(p_sb[sbuf_i][:], p_sb[sbuf_i][:],
                                      band_sb[:, off:off + 1024]),
                         wait=[(s_sc, exp_v[n])], inc=s_msk)
                    mask_v.append(p.val(s_msk))
                else:
                    mask_v.append(None)
                wpv = [(s_msk, mask_v[n])] if masked else [(s_sc, exp_v[n])]
                if kt == 0 and p.val(s_ctx) > 0:
                    wpv.append((s_ctx, p.val(s_ctx)))
                if pend is not None:
                    emit_pv(pend[0], pend[1], pend[2], nk)
                pend = (kt, sbuf_i, wpv)
            emit_pv(pend[0], pend[1], pend[2], nk)
            p.op("vector", lambda e: e.reciprocal(recip_sb[:],
                                                  ctx_ps[64:65, :]),
                 wait=[(s_pv, pv_v[-1])], inc=s_aux)
            p.op("scalar", lambda e: e.nop(), wait=[(s_aux, p.val(s_aux))],
                 inc=s_ack)
            rc_ack = p.val(s_ack)
            for half in range(2):
                p.op("tensor", lambda e, half=half:
                     e.matmul(rb_ps[:, half * 512:(half + 1) * 512],
                              ones64[:],
                              recip_sb[0:1, half * 512:(half + 1) * 512],
                              start=True, stop=True),
                     wait=([(s_ack, rc_ack)] if half == 0 else ()),
                     inc=(s_mm if half == 1 else None))
            p.op("scalar", lambda e: e.copy(rb_sb[:], rb_ps[:]),
                 wait=[(s_mm, p.val(s_mm))], inc=s_ack)
            cb = ctx_sb[(hh * NQG + qg) % 2]
            wctx = [(s_ack, p.val(s_ack))]
            if p.val(s_out) >= 16:
                wctx.append((s_out, p.val(s_out) - 16))
            p.op("vector", lambda e, cb=cb: e.tensor_mul(
                cb[:], ctx_ps[0:64, :], rb_sb[:]),
                wait=wctx, inc=s_ctx)
            p.dma("sync", o_ctx[hh * 64:(hh + 1) * 64,
                                qg * 1024:(qg + 1) * 1024], cb[:],
                  wait=[(s_ctx, p.val(s_ctx))], inc=s_out)
        pv_end_of_head[hh] = p.val(s_pv)
        if hh + 2 < HPC:
            load_head(hh + 2)

    return p



_IDENT = np.eye(128, dtype=np.float32)


def host_inputs_A(full_x, Wq, Wk, Wv):
    xf = full_x.reshape(B * S, D)
    maps = []
    for c in range(NCORES):
        rows = slice(c * RPC, (c + 1) * RPC)
        pos = (np.arange(c * RPC, (c + 1) * RPC) % S).astype(np.float32)
        maps.append({
            "x": f32(xf[rows]),
            "wqT": f32(Wq.T), "wkT": f32(Wk.T), "wvT": f32(Wv.T),
            "qpos": pos.reshape(1, RPC),
            "qfmod": f32(pos % 1024).reshape(1, RPC),
            "kfmod": f32(pos % 128).reshape(1, RPC),
            "identf": _IDENT,
        })
    return maps



def build_launch_C1():
    p = Prog()
    ctx_in = p.dram_in("ctx", [D, RPC], BF16)
    x_in = p.dram_in("x", [RPC, D])
    woT = p.dram_in("woT", [D, D])
    bo_in = p.dram_in("bo", [1, D])
    identf = p.dram_in("identf", [128, 128])
    o_x1 = p.dram_out("x1", [RPC, D])

    ctx_sb = p.sbuf("ctx_sb", [128, KD * RPC], BF16)
    w_sb = p.sbuf("w_sb", [128, KD * D], BF16)
    stage = p.sbuf("stage", [128, 4 * RPC])
    atok = p.sbuf("atok", [128, NT * D])
    xs = [p.sbuf(f"xs{j}", [128, D]) for j in range(2)]
    bo_sb = p.sbuf("bo_sb", [128, KD])
    identf_sb = p.sbuf("identf_sb", [128, 128])
    ln = LNBufs(p, "c1")
    pj = [p.psum(f"pj{j}", [128, 512]) for j in range(4)]
    tpf = [p.psum(f"tpf{j}", [128, 512]) for j in range(2)]

    s_in = p.sem("s_in")
    s_pj = p.sem("s_pj")
    s_ev = p.sem("s_ev")
    s_tb = p.sem("s_tb")
    s_tr = p.sem("s_tr")
    s_ln = p.sem("s_ln")
    s_x = p.sem("s_x")
    s_add = p.sem("s_add")
    s_st = p.sem("s_st")

    p.dma("sync", ctx_sb.ap().rearrange("p (k t) -> p k t", t=RPC),
          ctx_in.rearrange("(k p) t -> p k t", p=128), inc=s_in)
    p.dma("sync", identf_sb[:], identf[:], inc=s_in)
    p.dma("sync", bo_sb.ap().rearrange("p (k o) -> p k o", o=1),
          bo_in.rearrange("o (k p) -> p k o", p=128), inc=s_in,
          allow_slow_non_contiguous=True)
    p.dma("gpsimd", w_sb.ap().rearrange("p (k o) -> p k o", o=D),
          woT.rearrange("(k p) o -> p k o", p=128), inc=s_in)
    ldv = p.val(s_in)

    seq = 0
    ev_of = {}
    for oc in range(KD):
        for tt in range(RPC // 512):
            bank = pj[seq % 4]
            w = [(s_in, ldv)]
            if seq >= 4:
                w.append((s_ev, p.val(s_ev) - 3))
            if seq >= 8:
                w.append((s_tb, max(0, p.val(s_tb) - 2 * 16)))
            for k in range(KD):
                p.op("tensor", lambda e, oc=oc, tt=tt, k=k, bank=bank:
                     e.matmul(bank[:],
                              w_sb[:, k * D + oc * 128:k * D + (oc + 1) * 128],
                              ctx_sb[:, k * RPC + tt * 512:k * RPC + (tt + 1) * 512],
                              start=(k == 0), stop=(k == KD - 1)),
                     wait=(w if k == 0 else ()),
                     inc=(s_pj if k == KD - 1 else None))
            pv = p.val(s_pj)
            hoc = oc % 4
            p.op("vector", lambda e, oc=oc, tt=tt, hoc=hoc, bank=bank:
                 e.tensor_scalar(
                     stage[:, hoc * RPC + tt * 512:hoc * RPC + (tt + 1) * 512],
                     bank[:], bo_sb[:, oc:oc + 1], None, ALU.add),
                 wait=[(s_pj, pv)], inc=s_ev)
            ev_of[(oc, tt)] = p.val(s_ev)
            seq += 1
        if oc % 4 == 3:
            half = oc // 4
            for i in range(NT):
                bank2 = tpf[i % 2]
                wv = [(s_ev, ev_of[(oc, i // 4)])]
                if p.val(s_tb) >= 2:
                    wv.append((s_tr, p.val(s_tr) - 1))
                for j in range(4):
                    p.op("tensor", lambda e, i=i, j=j, half=half, bank2=bank2:
                         e.transpose(bank2[:, j * 128:(j + 1) * 128],
                                     stage[:, j * RPC + i * 128:
                                           j * RPC + (i + 1) * 128],
                                     identf_sb[:]),
                         wait=(wv if j == 0 else ()),
                         inc=(s_tb if j == 3 else None))
                tbv = p.val(s_tb)
                p.op("vector", lambda e, i=i, half=half, bank2=bank2:
                     e.tensor_copy(
                         out=atok[:, i * D + half * 512:i * D + (half + 1) * 512],
                         in_=bank2[:]),
                     wait=[(s_tb, tbv)], inc=s_tr)
    trv = p.val(s_tr)

    at = lambda i: atok[:, i * D:(i + 1) * D]
    emit_ln(p, ln, at, at, NT, lambda i: [(s_tr, trv)], s_ln)

    for i in range(NT):
        xb = xs[i % 2]
        w = [] if i < 2 else [(s_add, p.val(s_add) - 1)]
        p.dma("sync", xb[:], x_in.rearrange("(t p) d -> p t d", p=128)[:, i, :],
              wait=w, inc=s_x)
        xv = p.val(s_x)
        p.op("vector", lambda e, i=i, xb=xb: e.tensor_add(at(i), at(i), xb[:]),
             wait=[(s_x, xv), (s_ln, i + 1)], inc=s_add)
        p.dma("sync", o_x1.rearrange("(t p) d -> p t d", p=128)[:, i, :],
              at(i), wait=[(s_add, p.val(s_add))], inc=s_st)
    return p



def build_launch_C2():
    p = Prog()
    x1_in = p.dram_in("x1", [RPC, D])
    wqkv = p.dram_in("wqkv", [D, 3 * D])
    g_in = p.dram_in("ln1w", [1, D])
    b_in = p.dram_in("ln1b", [1, D])
    identf = p.dram_in("identf", [128, 128])
    o_y0 = p.dram_out("y0", [RPC, D])
    o_q = p.dram_out("q_msa", [D, RPC], BF16)
    o_k = p.dram_out("k_msa", [D, RPC], BF16)
    o_v = p.dram_out("v_msa", [RPC, H * 65], BF16)

    x_sb = p.sbuf("x_sb", [128, NT * D])
    xn_fm = p.sbuf("xn_fm", [128, KD * RPC], BF16)
    w_sb = p.sbuf("w_sb", [128, KD * 3 * D], BF16)
    stage = p.sbuf("stage", [128, KD * RPC], BF16)
    vrow = [p.sbuf(f"vrow{j}", [128, H * 65], BF16) for j in range(2)]
    gb_sb = p.sbuf("gb_sb", [128, 2 * D])
    grow = p.sbuf("grow", [1, 2 * D])
    identf_sb = p.sbuf("identf_sb", [128, 128])
    identb_sb = p.sbuf("identb_sb", [128, 128], BF16)
    ones1 = p.sbuf("ones1", [1, 128])
    ln = LNBufs(p, "c2")
    tpf = [p.psum(f"tpf{j}", [128, 512]) for j in range(2)]
    tpb = [p.psum(f"tpb{j}", [128, 1024], BF16) for j in range(2)]
    pj = [p.psum(f"pj{j}", [128, 512]) for j in range(4)]

    s_in = p.sem("s_in")
    s_ln = p.sem("s_ln")
    s_ln2 = p.sem("s_ln2")
    s_tb = p.sem("s_tb")
    s_tr = p.sem("s_tr")
    s_pj = p.sem("s_pj")
    s_ev = p.sem("s_ev")
    s_st = p.sem("s_st")
    s_vr = p.sem("s_vr")
    s_aux = p.sem("s_aux")
    s_ack = p.sem("s_ack")

    p.dma("sync", x_sb.ap().rearrange("p (t d) -> p t d", d=D),
          x1_in.rearrange("(t p) d -> p t d", p=128), inc=s_in)
    p.dma("sync", identf_sb[:], identf[:], inc=s_in)
    p.dma("sync", grow[0:1, 0:D], g_in[:], inc=s_in)
    p.dma("sync", grow[0:1, D:2 * D], b_in[:], inc=s_in)
    p.dma("gpsimd", w_sb.ap().rearrange("p (k o) -> p k o", o=3 * D),
          wqkv.rearrange("(k p) o -> p k o", p=128), inc=s_in)
    ldv = p.val(s_in)

    p.op("vector", lambda e: e.tensor_copy(out=identb_sb[:], in_=identf_sb[:]),
         wait=[(s_in, ldv)])
    p.op("vector", lambda e: e.memset(ones1[:], 1.0), inc=s_aux)
    p.op("scalar", lambda e: e.nop(), wait=[(s_aux, p.val(s_aux))], inc=s_ack)
    for j in range(4):
        p.op("tensor", lambda e, j=j: e.matmul(
            tpf[j % 2][:], ones1[:], grow[0:1, j * 512:(j + 1) * 512],
            start=True, stop=True),
            wait=[(s_ack, p.val(s_ack)), (s_in, ldv)] if j < 2 else
                 [(s_aux, p.val(s_aux))],
            inc=s_tb)
        tv = p.val(s_tb)
        p.op("vector", lambda e, j=j: e.tensor_copy(
            out=gb_sb[:, j * 512:(j + 1) * 512], in_=tpf[j % 2][:]),
            wait=[(s_tb, tv)], inc=s_aux)
    gbv = p.val(s_aux)

    xt = lambda i: x_sb[:, i * D:(i + 1) * D]
    emit_ln(p, ln, xt, xt, NT, lambda i: [(s_in, ldv)], s_ln)
    for i in range(NT):
        p.dma("sync", o_y0.rearrange("(t p) d -> p t d", p=128)[:, i, :],
              xt(i), wait=[(s_ln, i + 1)], inc=s_st)
    y0st = p.val(s_st)
    ln2b = LNBufs(p, "c2b")
    emit_ln(p, ln2b, xt, xt, NT,
            lambda i: [(s_st, y0st), (s_aux, gbv)], s_ln2,
            gamma=gb_sb[:, 0:D], beta=gb_sb[:, D:2 * D])

    def t_src(i, k):
        return x_sb[:, i * D + k * 128:i * D + (k + 1) * 128]

    def t_dst(i, k0):
        v = xn_fm.ap().rearrange("p (k t) -> p k t", t=RPC)
        return v[:, k0:k0 + 4, i * 128:(i + 1) * 128]

    emit_transpose(p, t_src, t_dst, KD, NT, identf_sb, tpf, s_tb, s_tr,
                   lambda i: [(s_ln2, i + 1), (s_aux, gbv)])
    trv = p.val(s_tr)

    seq = 0
    st_of = {}
    for oc in range(24):
        for tt in range(RPC // 512):
            bank = pj[seq % 4]
            w = [(s_tr, trv), (s_in, ldv)]
            if seq >= 4:
                w.append((s_ev, p.val(s_ev) - 3))
            for k in range(KD):
                p.op("tensor", lambda e, oc=oc, tt=tt, k=k, bank=bank:
                     e.matmul(bank[:],
                              w_sb[:, k * 3 * D + oc * 128:k * 3 * D + (oc + 1) * 128],
                              xn_fm[:, k * RPC + tt * 512:k * RPC + (tt + 1) * 512],
                              start=(k == 0), stop=(k == KD - 1)),
                     wait=(w if k == 0 else ()),
                     inc=(s_pj if k == KD - 1 else None))
            pv = p.val(s_pj)
            soc = oc % 8
            wev = [(s_pj, pv)]
            if oc >= 8:
                wev.append((s_st, st_of[oc - 8]))
            p.op("vector", lambda e, soc=soc, tt=tt, bank=bank: e.tensor_copy(
                out=stage[:, soc * RPC + tt * 512:soc * RPC + (tt + 1) * 512],
                in_=bank[:]), wait=wev, inc=s_ev)
            seq += 1
        if oc < 16:
            dst = o_q if oc < 8 else o_k
            soc = oc % 8
            p.dma("sync", dst[soc * 128:(soc + 1) * 128, :],
                  stage[:, soc * RPC:(soc + 1) * RPC],
                  wait=[(s_ev, p.val(s_ev))], inc=s_st)
        st_of[oc] = p.val(s_st)
    p.op("vector", lambda e: e.memset(vrow[0][:], 1.0))
    p.op("vector", lambda e: e.memset(vrow[1][:], 1.0), inc=s_vr)
    bank_last = [0, 0]
    for i in range(NT):
        vb = vrow[i % 2]
        for k0 in (0, 4):
            bank = tpb[k0 // 4]
            w0 = [(s_ev, p.val(s_ev))]
            if bank_last[k0 // 4] > 0:
                w0.append((s_vr, bank_last[k0 // 4]))
            if i >= 2 and k0 == 0:
                w0.append((s_st, p.val(s_st) - 16))
            for j in range(4):
                p.op("tensor", lambda e, i=i, j=j, k0=k0, bank=bank:
                     e.transpose(bank[:, j * 128:(j + 1) * 128],
                                 stage[:, (k0 + j) * RPC + i * 128:
                                       (k0 + j) * RPC + (i + 1) * 128],
                                 identb_sb[:]),
                     wait=(w0 if j == 0 else ()),
                     inc=(s_tb if j == 3 else None))
            tbv = p.val(s_tb)
            src3 = bank[:, 0:512].rearrange("p (h c) -> p h c", c=64)
            dst3 = vb.ap().rearrange("p (h c) -> p h c", c=65)
            p.op("vector", lambda e, k0=k0, src3=src3, dst3=dst3:
                 e.tensor_copy(out=dst3[:, 2 * k0:2 * k0 + 8, 0:64],
                               in_=src3[:, 0:8, :]),
                 wait=[(s_tb, tbv)], inc=s_vr)
            bank_last[k0 // 4] = p.val(s_vr)
        p.dma("sync", o_v[i * 128:(i + 1) * 128, :], vb[:],
              wait=[(s_vr, p.val(s_vr))], inc=s_st)
    return p


_BAND = None
def band_const():
    global _BAND
    if _BAND is None:
        jj = np.arange(1920)[None, :]
        kk = np.arange(128)[:, None]
        _BAND = ((jj - 896) >= kk).astype(ml_dtypes.bfloat16)
    return _BAND


def biastab_const(core):
    hg = core % 4
    t = np.zeros((128, HPC * 64), np.float32)
    for hl in range(HPC):
        h = 4 * hg + hl
        jv = np.arange(64)
        t[:, hl * 64:(hl + 1) * 64] = (-128.0 * (jv - 7) * SLOPES[h] / 240.0)[None, :]
    return t


def host_inputs_B(resA):
    maps = []
    for c in range(NCORES):
        b, hg = c // 4, c % 4
        src_cores = [4 * b + j for j in range(4)]
        q = np.concatenate([np.concatenate(
            [np.asarray(resA[sc]["q_ext"])[67 * (4 * hg + hl):67 * (4 * hg + hl + 1)]
             for sc in src_cores], axis=1) for hl in range(HPC)], axis=0)
        k = np.concatenate([np.concatenate(
            [np.asarray(resA[sc]["k_ext"])[67 * (4 * hg + hl):67 * (4 * hg + hl + 1)]
             for sc in src_cores], axis=1) for hl in range(HPC)], axis=0)
        v = np.concatenate([np.concatenate(
            [np.asarray(resA[sc]["v_tok"])[:, 65 * (4 * hg + hl):65 * (4 * hg + hl + 1)]
             for sc in src_cores], axis=0) for hl in range(HPC)], axis=1)
        maps.append({"q": q, "k": k, "v": v,
                     "biastab": biastab_const(c), "band": band_const()})
    return maps



def build_launch_D():
    p = Prog()
    q_in = p.dram_in("q", [HPC * 64, S], BF16)
    k_in = p.dram_in("k", [HPC * 64, S], BF16)
    v_in = p.dram_in("v", [S, HPC * 65], BF16)
    o_ctx = p.dram_out("ctx", [HPC * 64, S], BF16)

    NKT = S // 128
    NQG = S // 1024

    q_sb = [p.sbuf(f"q{j}", [65, S], BF16) for j in range(2)]
    k_sb = [p.sbuf(f"k{j}", [65, S], BF16) for j in range(2)]
    v_sb = [p.sbuf(f"v{j}", [128, NKT * 65], BF16) for j in range(2)]
    sq_sb = p.sbuf("sq_sb", [64, S], BF16)
    r_sb = p.sbuf("r_sb", [1, S])
    km = p.sbuf("km", [1, 1])
    onesb = p.sbuf("onesb", [64, 1], BF16)
    ones64 = p.sbuf("ones64", [1, 64])
    p_sb = [p.sbuf(f"pp{j}", [128, 1024], BF16) for j in range(2)]
    recip_sb = p.sbuf("recip_sb", [1, 1024])
    rb_sb = p.sbuf("rb_sb", [64, 1024])
    ctx_sb = [p.sbuf(f"ctx_sb{j}", [64, 1024], BF16) for j in range(2)]

    sc_ps = [p.psum(f"scps{j}", [128, 1024]) for j in range(2)]
    ctx_ps = p.psum("ctxps", [65, 1024])
    rb_ps = p.psum("rbps", [64, 1024])

    s_ld = p.sem("s_ld")
    s_mm = p.sem("s_mm")
    s_sc = p.sem("s_sc")
    s_pv = p.sem("s_pv")
    s_ctx = p.sem("s_ctx")
    s_aux = p.sem("s_aux")
    s_ack = p.sem("s_ack")
    s_out = p.sem("s_out")

    p.op("vector", lambda e: e.memset(onesb[:], 1.0))
    p.op("vector", lambda e: e.memset(ones64[:], 1.0))
    p.op("vector", lambda e: e.memset(k_sb[0][64:65, :], 1.0))
    p.op("vector", lambda e: e.memset(k_sb[1][64:65, :], 1.0), inc=s_aux)
    p.op("scalar", lambda e: e.nop(), wait=[(s_aux, p.val(s_aux))], inc=s_ack)
    cst_ack = p.val(s_ack)

    pv_end_of_head = {}
    ld_done = {}

    def load_head(hh):
        buf = hh % 2
        w = [] if hh < 2 else [(s_pv, pv_end_of_head[hh - 2])]
        p.dma("sync", q_sb[buf][0:64, :], q_in[hh * 64:(hh + 1) * 64, :],
              wait=w, inc=s_ld)
        p.dma("sync", k_sb[buf][0:64, :], k_in[hh * 64:(hh + 1) * 64, :],
              inc=s_ld)
        p.dma("sync", v_sb[buf].ap().rearrange("p (k c) -> p k c", c=65),
              v_in.rearrange("(k p) c -> p k c", p=128)[:, :, hh * 65:(hh + 1) * 65],
              inc=s_ld)
        ld_done[hh] = p.val(s_ld)

    load_head(0)
    load_head(1)

    exp_v = []
    pv_v = []

    for hh in range(HPC):
        buf = hh % 2
        p.op("scalar", lambda e, buf=buf: e.activation(
            sq_sb[:], k_sb[buf][0:64, :], AF.Square),
            wait=[(s_ld, ld_done[hh]), (s_ack, cst_ack)], inc=s_ack)
        for t8 in range(S // 512):
            p.op("tensor", lambda e, t8=t8: e.matmul(
                rb_ps[0:1, 0:512], onesb[:],
                sq_sb[:, t8 * 512:(t8 + 1) * 512], start=True, stop=True),
                wait=[(s_ack, p.val(s_ack)),
                      (s_ctx, p.val(s_ctx))] if t8 == 0 else
                     [(s_aux, p.val(s_aux))],
                inc=s_mm)
            p.op("vector", lambda e, t8=t8: e.tensor_copy(
                out=r_sb[0:1, t8 * 512:(t8 + 1) * 512], in_=rb_ps[0:1, 0:512]),
                wait=[(s_mm, p.val(s_mm))], inc=s_aux)
        p.op("vector", lambda e: e.reduce_max(km[:], r_sb[:], AX.X),
             wait=[(s_aux, p.val(s_aux))], inc=s_aux)
        p.op("scalar", lambda e, buf=buf: e.activation(
            sq_sb[:], q_sb[buf][0:64, :], AF.Square),
            wait=[(s_aux, p.val(s_aux))], inc=s_ack)
        for t8 in range(S // 512):
            p.op("tensor", lambda e, t8=t8: e.matmul(
                rb_ps[0:1, 0:512], onesb[:],
                sq_sb[:, t8 * 512:(t8 + 1) * 512], start=True, stop=True),
                wait=[(s_ack, p.val(s_ack))] if t8 == 0 else
                     [(s_aux, p.val(s_aux))],
                inc=s_mm)
            p.op("vector", lambda e, t8=t8: e.tensor_copy(
                out=r_sb[0:1, t8 * 512:(t8 + 1) * 512], in_=rb_ps[0:1, 0:512]),
                wait=[(s_mm, p.val(s_mm))], inc=s_aux)
        p.op("scalar", lambda e: e.nop(), wait=[(s_aux, p.val(s_aux))],
             inc=s_ack)
        p.op("vector", lambda e: e.tensor_scalar(
            r_sb[:], r_sb[:], km[0:1, 0:1], None, ALU.mult),
            wait=[(s_ack, p.val(s_ack))], inc=s_aux)
        p.op("scalar", lambda e: e.activation(r_sb[:], r_sb[:], AF.Sqrt),
             wait=[(s_aux, p.val(s_aux))], inc=s_ack)
        p.op("vector", lambda e, buf=buf: e.tensor_scalar_mul(
            q_sb[buf][64:65, :], r_sb[:], -8.0),
            wait=[(s_ack, p.val(s_ack))], inc=s_aux)
        p.op("scalar", lambda e: e.nop(), wait=[(s_aux, p.val(s_aux))],
             inc=s_ack)
        row_ack = p.val(s_ack)

        for qg in range(NQG):
            pend = None

            def emit_pv(kt, sbuf_i, wpv, buf=buf):
                for half in range(2):
                    p.op("tensor", lambda e, kt=kt, half=half, sbuf_i=sbuf_i,
                         buf=buf:
                         e.matmul(ctx_ps[:, half * 512:(half + 1) * 512],
                                  v_sb[buf][:, kt * 65:(kt + 1) * 65],
                                  p_sb[sbuf_i][:, half * 512:(half + 1) * 512],
                                  start=(kt == 0), stop=(kt == NKT - 1)),
                         wait=(wpv if half == 0 else ()),
                         inc=(s_pv if half == 1 else None))
                pv_v.append(p.val(s_pv))

            for kt in range(NKT):
                n = len(exp_v)
                sbuf_i = n % 2
                w0 = [(s_ld, ld_done[hh]), (s_ack, row_ack)]
                if n >= 2:
                    w0.append((s_sc, exp_v[n - 2]))
                for half in range(2):
                    p.op("tensor", lambda e, kt=kt, qg=qg, half=half,
                         sbuf_i=sbuf_i, buf=buf:
                         e.matmul(sc_ps[sbuf_i][:, half * 512:(half + 1) * 512],
                                  k_sb[buf][:, kt * 128:(kt + 1) * 128],
                                  q_sb[buf][:, qg * 1024 + half * 512:
                                            qg * 1024 + (half + 1) * 512],
                                  start=True, stop=True),
                         wait=(w0 if half == 0 else ()),
                         inc=(s_mm if half == 1 else None))
                mmv = p.val(s_mm)
                wexp = [(s_mm, mmv)]
                if n >= 2:
                    wexp.append((s_pv, pv_v[n - 2]))
                p.op("scalar", lambda e, sbuf_i=sbuf_i:
                     e.activation(p_sb[sbuf_i][:], sc_ps[sbuf_i][:],
                                  AF.Exp, scale=0.125),
                     wait=wexp, inc=s_sc)
                exp_v.append(p.val(s_sc))
                wpv = [(s_sc, exp_v[n])]
                if kt == 0 and p.val(s_ctx) > 0:
                    wpv.append((s_ctx, p.val(s_ctx)))
                if pend is not None:
                    emit_pv(pend[0], pend[1], pend[2])
                pend = (kt, sbuf_i, wpv)
            emit_pv(pend[0], pend[1], pend[2])
            p.op("vector", lambda e: e.reciprocal(recip_sb[:],
                                                  ctx_ps[64:65, :]),
                 wait=[(s_pv, pv_v[-1])], inc=s_aux)
            p.op("scalar", lambda e: e.nop(), wait=[(s_aux, p.val(s_aux))],
                 inc=s_ack)
            rc_ack = p.val(s_ack)
            for half in range(2):
                p.op("tensor", lambda e, half=half:
                     e.matmul(rb_ps[:, half * 512:(half + 1) * 512],
                              ones64[:],
                              recip_sb[0:1, half * 512:(half + 1) * 512],
                              start=True, stop=True),
                     wait=([(s_ack, rc_ack)] if half == 0 else ()),
                     inc=(s_mm if half == 1 else None))
            p.op("scalar", lambda e: e.copy(rb_sb[:], rb_ps[:]),
                 wait=[(s_mm, p.val(s_mm))], inc=s_ack)
            cb = ctx_sb[(hh * NQG + qg) % 2]
            wctx = [(s_ack, p.val(s_ack))]
            if p.val(s_out) >= 16:
                wctx.append((s_out, p.val(s_out) - 16))
            p.op("vector", lambda e, cb=cb: e.tensor_mul(
                cb[:], ctx_ps[0:64, :], rb_sb[:]),
                wait=wctx, inc=s_ctx)
            p.dma("sync", o_ctx[hh * 64:(hh + 1) * 64,
                                qg * 1024:(qg + 1) * 1024], cb[:],
                  wait=[(s_ctx, p.val(s_ctx))], inc=s_out)
        pv_end_of_head[hh] = p.val(s_pv)
        if hh + 2 < HPC:
            load_head(hh + 2)

    return p


def host_inputs_D(resC2):
    maps = []
    for c in range(NCORES):
        b, hg = c // 4, c % 4
        src_cores = [4 * b + j for j in range(4)]
        def cat(key, rows_of):
            return np.concatenate([np.concatenate(
                [np.asarray(resC2[sc][key])[rows_of(hl)] for sc in src_cores],
                axis=1) for hl in range(HPC)], axis=0)
        q = cat("q_msa", lambda hl: slice(64 * (4 * hg + hl),
                                          64 * (4 * hg + hl + 1)))
        k = cat("k_msa", lambda hl: slice(64 * (4 * hg + hl),
                                          64 * (4 * hg + hl + 1)))
        v = np.concatenate([np.concatenate(
            [np.asarray(resC2[sc]["v_msa"])[:, 65 * (4 * hg + hl):
                                            65 * (4 * hg + hl + 1)]
             for sc in src_cores], axis=0) for hl in range(HPC)], axis=1)
        maps.append({"q": q, "k": k, "v": v})
    return maps



def build_launch_E1():
    p = Prog()
    ctx_in = p.dram_in("ctx", [D, RPC], BF16)
    y0_in = p.dram_in("y0", [RPC, D])
    wproj = p.dram_in("wproj", [D, D])
    g_in = p.dram_in("ln2w", [1, D])
    b_in = p.dram_in("ln2b", [1, D])
    gw_in = p.dram_in("gatew", [D, 8])
    gb_in = p.dram_in("gateb", [1, 8])
    identf = p.dram_in("identf", [128, 128])
    o_y1 = p.dram_out("y1", [RPC, D])
    o_z = p.dram_out("z_fm", [D, RPC], BF16)
    o_m = p.dram_out("mask", [8, RPC], BF16)

    ctx_sb = p.sbuf("ctx_sb", [128, KD * RPC], BF16)
    w_sb = p.sbuf("w_sb", [128, KD * D], BF16)
    gw_sb = p.sbuf("gw_sb", [128, KD * 8])
    gb_sb = p.sbuf("gb_sb", [1, 8])
    stage = p.sbuf("stage", [128, 8 * 512])
    wt = [p.sbuf(f"wt{j}", [128, D]) for j in range(2)]
    ys = [p.sbuf(f"ys{j}", [128, D]) for j in range(2)]
    zf32 = [p.sbuf(f"zf32_{j}", [128, 8 * 128]) for j in range(2)]
    z_fm = p.sbuf("z_fm_sb", [128, KD * RPC], BF16)
    gbc_sb = p.sbuf("gbc_sb", [128, 2 * D])
    grow = p.sbuf("grow", [1, 2 * D])
    onesc = p.sbuf("onesc", [1, 128])
    mxbuf = p.sbuf("mxbuf", [128, 1])
    mask_sb = p.sbuf("mask_sb", [128, NT * 8])
    mfm_sb = p.sbuf("mfm_sb", [8, RPC], BF16)
    identf_sb = p.sbuf("identf_sb", [128, 128])
    ln = LNBufs(p, "e1")
    pj = [p.psum(f"pj{j}", [128, 512]) for j in range(2)]
    tpf = [p.psum(f"tpf{j}", [128, 512]) for j in range(2)]
    pg = p.psum("pg", [128, 512])

    s_in = p.sem("s_in")
    s_pj = p.sem("s_pj")
    s_ev = p.sem("s_ev")
    s_tb = p.sem("s_tb")
    s_tr = p.sem("s_tr")
    s_y = p.sem("s_y")
    s_ln = p.sem("s_ln")
    s_st = p.sem("s_st")
    s_aux = p.sem("s_aux")
    s_ack = p.sem("s_ack")
    s_g = p.sem("s_g")
    s_add1 = p.sem("s_add1")
    s_sq1 = p.sem("s_sq1")

    p.dma("sync", ctx_sb.ap().rearrange("p (k t) -> p k t", t=RPC),
          ctx_in.rearrange("(k p) t -> p k t", p=128), inc=s_in)
    p.dma("sync", identf_sb[:], identf[:], inc=s_in)
    p.dma("sync", grow[0:1, 0:D], g_in[:], inc=s_in)
    p.dma("sync", grow[0:1, D:2 * D], b_in[:], inc=s_in)
    p.dma("sync", gw_sb.ap().rearrange("p (k o) -> p k o", o=8),
          gw_in.rearrange("(k p) o -> p k o", p=128), inc=s_in)
    p.dma("sync", gb_sb[:], gb_in[:], inc=s_in)
    p.dma("gpsimd", w_sb.ap().rearrange("p (k o) -> p k o", o=D),
          wproj.rearrange("(k p) o -> p k o", p=128), inc=s_in)
    ldv = p.val(s_in)

    p.op("vector", lambda e: e.memset(onesc[:], 1.0), wait=[(s_in, ldv)],
         inc=s_aux)
    p.op("scalar", lambda e: e.nop(), wait=[(s_aux, p.val(s_aux))], inc=s_ack)
    for j in range(4):
        p.op("tensor", lambda e, j=j: e.matmul(
            tpf[j % 2][:], onesc[:], grow[0:1, j * 512:(j + 1) * 512],
            start=True, stop=True),
            wait=[(s_ack, p.val(s_ack)), (s_in, ldv)] if j < 2 else
                 [(s_aux, p.val(s_aux))],
            inc=s_tb)
        tv = p.val(s_tb)
        p.op("vector", lambda e, j=j: e.tensor_copy(
            out=gbc_sb[:, j * 512:(j + 1) * 512], in_=tpf[j % 2][:]),
            wait=[(s_tb, tv)], inc=s_aux)
    gbv = p.val(s_aux)

    inv_d = 1.0 / D
    ln.eps_done = True
    p.op("vector", lambda e: e.memset(ln.epsb[:], EPS))
    for tt in range(4):
        for oc in range(KD):
            bank = pj[oc % 2]
            w = [(s_in, ldv)]
            if tt * KD + oc >= 2:
                w.append((s_ev, p.val(s_ev) - 1))
            if tt > 0 and oc == 0:
                w.append((s_tb, gbv * 0 + p.val(s_tb)))
            for k in range(KD):
                p.op("tensor", lambda e, oc=oc, tt=tt, k=k, bank=bank:
                     e.matmul(bank[:],
                              w_sb[:, k * D + oc * 128:k * D + (oc + 1) * 128],
                              ctx_sb[:, k * RPC + tt * 512:k * RPC + (tt + 1) * 512],
                              start=(k == 0), stop=(k == KD - 1)),
                     wait=(w if k == 0 else ()),
                     inc=(s_pj if k == KD - 1 else None))
            pv = p.val(s_pj)
            p.op("vector", lambda e, oc=oc, bank=bank: e.tensor_copy(
                out=stage[:, oc * 512:(oc + 1) * 512], in_=bank[:]),
                wait=[(s_pj, pv)], inc=s_ev)
        evv = p.val(s_ev)
        for i4 in range(4):
            i = tt * 4 + i4
            w_tile = wt[i % 2]
            yb = ys[i % 2]
            for k0 in (0, 4):
                bank2 = tpf[k0 // 4]
                wv = [(s_ev, evv)]
                if p.val(s_tr) >= 2:
                    wv.append((s_tr, p.val(s_tr) - 1))
                if i >= 2 and k0 == 0:
                    wv.append((s_sq1, i - 1))
                    wv.append((s_st, 16 * (i - 1)))
                for j in range(4):
                    p.op("tensor", lambda e, i4=i4, j=j, k0=k0, bank2=bank2:
                         e.transpose(bank2[:, j * 128:(j + 1) * 128],
                                     stage[:, (k0 + j) * 512 + i4 * 128:
                                           (k0 + j) * 512 + (i4 + 1) * 128],
                                     identf_sb[:]),
                         wait=(wv if j == 0 else ()),
                         inc=(s_tb if j == 3 else None))
                tbv = p.val(s_tb)
                p.op("vector", lambda e, k0=k0, w_tile=w_tile, bank2=bank2:
                     e.tensor_copy(out=w_tile[:, k0 * 128:(k0 + 4) * 128],
                                   in_=bank2[:]),
                     wait=[(s_tb, tbv)], inc=s_tr)
            trv = p.val(s_tr)
            wy = [] if i < 2 else [(s_add1, i - 1)]
            p.dma("sync", yb[:],
                  y0_in.rearrange("(t p) d -> p t d", p=128)[:, i, :],
                  wait=wy, inc=s_y)
            p.op("vector", lambda e, w_tile=w_tile, yb=yb: e.tensor_add(
                w_tile[:], w_tile[:], yb[:]),
                wait=[(s_y, p.val(s_y)), (s_tr, trv)], inc=s_add1)
            av = p.val(s_add1)
            p.dma("sync", o_y1.rearrange("(t p) d -> p t d", p=128)[:, i, :],
                  w_tile[:], wait=[(s_add1, av)], inc=s_st)
            p.op("vector", lambda e, i=i, w_tile=w_tile: e.reduce_sum(
                ln.msum[:, i:i + 1], w_tile[:], AX.X))
            p.op("scalar", lambda e, i=i, w_tile=w_tile: e.activation(
                ln.sc[:], w_tile[:], AF.Square,
                accum_out=ln.sqs[:, i:i + 1]),
                wait=[(s_add1, av)], inc=s_sq1)
    p.op("vector", lambda e: e.engine_nop(), wait=[(s_sq1, NT)], inc=s_aux)
    p.op("scalar", lambda e: e.nop(), wait=[(s_aux, p.val(s_aux))], inc=s_ack)
    p.op("vector", lambda e: e.tensor_scalar_mul(
        ln.mean[:, 0:NT], ln.msum[:, 0:NT], inv_d),
        wait=[(s_ack, p.val(s_ack))], inc=s_aux)
    p.op("scalar", lambda e: e.nop(), wait=[(s_aux, p.val(s_aux))], inc=s_ack)
    p.op("vector", lambda e: e.tensor_mul(
        ln.m2[:, 0:NT], ln.mean[:, 0:NT], ln.mean[:, 0:NT]),
        wait=[(s_ack, p.val(s_ack))], inc=s_aux)
    p.op("scalar", lambda e: e.nop(), wait=[(s_aux, p.val(s_aux))], inc=s_ack)
    p.op("vector", lambda e: e.scalar_tensor_tensor(
        ln.vr[:, 0:NT], ln.sqs[:, 0:NT], inv_d, ln.m2[:, 0:NT],
        ALU.mult, ALU.subtract), wait=[(s_ack, p.val(s_ack))], inc=s_aux)
    p.op("scalar", lambda e: e.activation(
        ln.vln[:, 0:NT], ln.vr[:, 0:NT], AF.Ln, bias=ln.epsb[:]),
        wait=[(s_aux, p.val(s_aux))], inc=s_ack)
    p.op("vector", lambda e: e.engine_nop(), wait=[(s_ack, p.val(s_ack))],
         inc=s_aux)
    p.op("scalar", lambda e: e.activation(
        ln.rstd[:, 0:NT], ln.vln[:, 0:NT], AF.Exp, scale=-0.5),
        wait=[(s_aux, p.val(s_aux))], inc=s_ack)
    rstd_ack = p.val(s_ack)

    gcnt = 0
    for i in range(NT):
        yb = ys[i % 2]
        w_tile = wt[i % 2]
        wy = [(s_st, 16 * NT)] if i < 2 else [(s_g, p.val(s_g))]
        p.dma("sync", yb[:], o_y1.rearrange("(t p) d -> p t d", p=128)[:, i, :],
              wait=wy, inc=s_y)
        p.op("vector", lambda e, i=i, yb=yb, w_tile=w_tile: e.tensor_scalar(
            w_tile[:], yb[:], ln.mean[:, i:i + 1], ln.rstd[:, i:i + 1],
            ALU.subtract, ALU.mult),
            wait=[(s_y, p.val(s_y)), (s_ack, rstd_ack)])
        p.op("vector", lambda e, w_tile=w_tile: e.tensor_mul(
            w_tile[:], w_tile[:], gbc_sb[:, 0:D]))
        p.op("vector", lambda e, w_tile=w_tile: e.tensor_add(
            w_tile[:], w_tile[:], gbc_sb[:, D:2 * D]), inc=s_aux)
        zv = p.val(s_aux)
        zb = zf32[i % 2]
        for k0 in (0, 4):
            bank2 = tpf[k0 // 4]
            wv = [(s_aux, zv)]
            if p.val(s_tr) >= 2:
                wv.append((s_tr, p.val(s_tr) - 1))
            for j in range(4):
                p.op("tensor", lambda e, j=j, k0=k0, w_tile=w_tile, bank2=bank2:
                     e.transpose(bank2[:, j * 128:(j + 1) * 128],
                                 w_tile[:, (k0 + j) * 128:(k0 + j + 1) * 128],
                                 identf_sb[:]),
                     wait=(wv if j == 0 else ()),
                     inc=(s_tb if j == 3 else None))
            tbv = p.val(s_tb)
            zv_dst = z_fm.ap().rearrange("p (k t) -> p k t", t=RPC)
            p.op("vector", lambda e, i=i, k0=k0, bank2=bank2, zv_dst=zv_dst:
                 e.tensor_copy(out=zv_dst[:, k0:k0 + 4, i * 128:(i + 1) * 128],
                               in_=bank2.ap().rearrange("p (k t) -> p k t",
                                                        t=128)),
                 wait=[(s_tb, tbv)], inc=s_tr)
            p.op("vector", lambda e, k0=k0, zb=zb, bank2=bank2:
                 e.tensor_copy(out=zb[:, k0 * 128:(k0 + 4) * 128],
                               in_=bank2[:]), inc=s_tr)
        trv = p.val(s_tr)
        for k in range(KD):
            p.op("tensor", lambda e, k=k, zb=zb: e.matmul(
                pg[:, 0:8], zb[:, k * 128:(k + 1) * 128],
                gw_sb[:, k * 8:(k + 1) * 8], start=(k == 0), stop=False),
                wait=([(s_tr, trv), (s_g, gcnt)] if k == 0 else ()))
        p.op("tensor", lambda e: e.matmul(
            pg[:, 0:8], onesc[:, 0:128], gb_sb[:], start=False, stop=True),
            inc=s_pj)
        gv = p.val(s_pj)
        p.op("vector", lambda e: e.reduce_max(mxbuf[:], pg[:, 0:8], AX.X),
             wait=[(s_pj, gv)], inc=s_aux)
        p.op("scalar", lambda e: e.nop(), wait=[(s_aux, p.val(s_aux))],
             inc=s_ack)
        p.op("vector", lambda e, i=i: e.tensor_scalar(
            mask_sb[:, i * 8:(i + 1) * 8], pg[:, 0:8], mxbuf[:], None,
            ALU.is_equal), wait=[(s_ack, p.val(s_ack))], inc=s_g)
        gcnt = p.val(s_g)
    for i in range(NT):
        bank2 = tpf[i % 2]
        wv = [(s_g, gcnt)]
        if i >= 2:
            wv.append((s_tr, p.val(s_tr) - 1))
        p.op("tensor", lambda e, i=i, bank2=bank2: e.transpose(
            bank2[0:8, 0:128], mask_sb[:, i * 8:(i + 1) * 8], identf_sb[:]),
            wait=wv, inc=s_tb)
        tbv = p.val(s_tb)
        p.op("vector", lambda e, i=i, bank2=bank2: e.tensor_copy(
            out=mfm_sb[:, i * 128:(i + 1) * 128], in_=bank2[0:8, 0:128]),
            wait=[(s_tb, tbv)], inc=s_tr)
    p.dma("sync", o_m[:], mfm_sb[:],
          wait=[(s_tr, p.val(s_tr))], inc=s_st)
    p.dma("sync", o_z.rearrange("(k p) t -> p k t", p=128),
          z_fm.ap().rearrange("p (k t) -> p k t", t=RPC),
          wait=[(s_tr, p.val(s_tr))], inc=s_st)
    return p



def build_launch_E2():
    p = Prog()
    z_in = p.dram_in("z_fm", [D, RPC], BF16)
    m_in = p.dram_in("mask", [8, RPC], BF16)
    w1_in = p.dram_in("w1", [8, D, 4096])
    w2_in = p.dram_in("w2", [8, 4096, D])
    o_moe = p.dram_out("moe_fm", [D, RPC], BF16)

    HID = 4096
    NHC = HID // 128

    z_sb = p.sbuf("z_sb", [128, KD * RPC], BF16)
    w1_sb = p.sbuf("w1_sb", [128, KD * HID], BF16)
    w2_sb = [p.sbuf(f"w2_{j}", [128, NHC * 128], BF16) for j in range(2)]
    h_sb = p.sbuf("h_sb", [128, NHC * 512], BF16)
    ze_sb = p.sbuf("ze_sb", [128, KD * 512], BF16)
    mrow_sb = p.sbuf("mrow_sb", [1, 2 * RPC], BF16)
    mbc_sb = p.sbuf("mbc_sb", [128, 512], BF16)
    moe_sb = p.sbuf("moe_sb", [128, KD * RPC], BF16)
    onesc_b = p.sbuf("onesc_b", [1, 128], BF16)

    ph = [p.psum(f"ph{j}", [128, 512]) for j in range(2)]
    po = [p.psum(f"po{j}", [128, 512]) for j in range(2)]
    pm = p.psum("pm", [128, 512])

    s_in = p.sem("s_in")
    s_w = p.sem("s_w")
    s_h = p.sem("s_h")
    s_hm = p.sem("s_hm")
    s_om = p.sem("s_om")
    s_oe = p.sem("s_oe")
    s_ze = p.sem("s_ze")
    s_mb = p.sem("s_mb")
    s_aux = p.sem("s_aux")
    s_ack = p.sem("s_ack")
    s_st = p.sem("s_st")

    p.dma("sync", z_sb.ap().rearrange("p (k t) -> p k t", t=RPC),
          z_in.rearrange("(k p) t -> p k t", p=128), inc=s_in)
    ldv = p.val(s_in)
    p.op("vector", lambda e: e.memset(onesc_b[:], 1.0), wait=[(s_in, ldv)])
    p.op("vector", lambda e: e.memset(moe_sb[:], 0.0), inc=s_aux)
    p.op("scalar", lambda e: e.nop(), wait=[(s_aux, p.val(s_aux))], inc=s_ack)
    cst_ack = p.val(s_ack)

    mfm_ack = cst_ack

    def load_w1(e_):
        w = [(s_hm, p.val(s_hm))] if e_ > 0 else []
        p.dma("gpsimd", w1_sb.ap().rearrange("p (k o) -> p k o", o=HID),
              w1_in[e_].rearrange("(k p) o -> p k o", p=128), wait=w, inc=s_w)
        return p.val(s_w)

    w2_last = [0, 0]

    def load_w2(e_, oc, buf):
        w = [(s_om, w2_last[buf])] if w2_last[buf] > 0 else []
        p.dma("gpsimd", w2_sb[buf].ap().rearrange("p (k o) -> p k o", o=128),
              w2_in[e_].rearrange("(k p) o -> p k o", p=128)[:, :, oc * 128:(oc + 1) * 128],
              wait=w, inc=s_w)
        return p.val(s_w)

    mrow_v = {}
    for e_ in range(8):
        w1v = load_w1(e_)
        wm = [] if e_ < 2 else [(s_hm, p.val(s_hm))]
        p.dma("sync", mrow_sb[0:1, (e_ % 2) * RPC:(e_ % 2 + 1) * RPC],
              m_in[e_:e_ + 1, :], wait=wm, inc=s_in)
        mrow_v[e_] = p.val(s_in)
        for tt in range(4):
            p.op("tensor", lambda e, e_=e_, tt=tt: e.matmul(
                pm[:], onesc_b[:],
                mrow_sb[0:1, (e_ % 2) * RPC + tt * 512:
                        (e_ % 2) * RPC + (tt + 1) * 512],
                start=True, stop=True),
                wait=[(s_ack, mfm_ack), (s_ze, p.val(s_ze)),
                      (s_in, mrow_v[e_])], inc=s_hm)
            p.op("vector", lambda e: e.tensor_copy(out=mbc_sb[:], in_=pm[:]),
                 wait=[(s_hm, p.val(s_hm))], inc=s_mb)
            mbv = p.val(s_mb)
            for k in range(KD):
                p.op("vector", lambda e, k=k, tt=tt: e.tensor_mul(
                    ze_sb[:, k * 512:(k + 1) * 512],
                    z_sb[:, k * RPC + tt * 512:k * RPC + (tt + 1) * 512],
                    mbc_sb[:]),
                    wait=([(s_mb, mbv), (s_hm, p.val(s_hm))] if k == 0 else ()),
                    inc=(s_ze if k == KD - 1 else None))
            zev = p.val(s_ze)
            om_prev = p.val(s_om)
            for hc in range(NHC):
                bank = ph[hc % 2]
                wmm = [(s_ze, zev), (s_w, w1v)]
                if hc >= 2:
                    wmm.append((s_h, p.val(s_h) - 1))
                elif p.val(s_h) > 0:
                    wmm.append((s_h, p.val(s_h)))
                for k in range(KD):
                    p.op("tensor", lambda e, hc=hc, k=k, bank=bank:
                         e.matmul(bank[:],
                                  w1_sb[:, k * HID + hc * 128:k * HID + (hc + 1) * 128],
                                  ze_sb[:, k * 512:(k + 1) * 512],
                                  start=(k == 0), stop=(k == KD - 1)),
                         wait=(wmm if k == 0 else ()),
                         inc=(s_hm if k == KD - 1 else None))
                wg = [(s_hm, p.val(s_hm))]
                if hc == 0 and om_prev > 0:
                    wg.append((s_om, om_prev))
                p.op("scalar", lambda e, hc=hc, bank=bank: e.activation(
                    h_sb[:, hc * 512:(hc + 1) * 512], bank[:],
                    AF.Gelu_apprx_tanh),
                    wait=wg, inc=s_h)
            hv = p.val(s_h)
            for oc in range(KD):
                buf = oc % 2
                w2v = load_w2(e_, oc, buf)
                bank = po[oc % 2]
                wmm = [(s_h, hv), (s_w, w2v)]
                if p.val(s_oe) >= 2:
                    wmm.append((s_oe, p.val(s_oe) - 1))
                for k in range(NHC):
                    p.op("tensor", lambda e, oc=oc, k=k, bank=bank, buf=buf:
                         e.matmul(bank[:],
                                  w2_sb[buf][:, k * 128:(k + 1) * 128],
                                  h_sb[:, k * 512:(k + 1) * 512],
                                  start=(k == 0), stop=(k == NHC - 1)),
                         wait=(wmm if k == 0 else ()),
                         inc=(s_om if k == NHC - 1 else None))
                omv = p.val(s_om)
                w2_last[buf] = omv
                p.op("vector", lambda e, oc=oc, tt=tt, bank=bank:
                     e.tensor_add(
                         moe_sb[:, oc * RPC + tt * 512:oc * RPC + (tt + 1) * 512],
                         moe_sb[:, oc * RPC + tt * 512:oc * RPC + (tt + 1) * 512],
                         bank[:]),
                     wait=[(s_om, omv)], inc=s_oe)
    p.dma("sync", o_moe.rearrange("(k p) t -> p k t", p=128),
          moe_sb.ap().rearrange("p (k t) -> p k t", t=RPC),
          wait=[(s_oe, p.val(s_oe))], inc=s_st)
    return p





def build_launch_E3(debug=False):
    p = Prog()
    moe_in = p.dram_in("moe_fm", [D, RPC], BF16)
    y1_in = p.dram_in("y1", [RPC, D])
    x1_in = p.dram_in("x1", [RPC, D])
    identb = p.dram_in("identb", [128, 128], BF16)
    o_x2 = p.dram_out("x2", [RPC, D])
    if debug:
        o_y2 = p.dram_out("y2dbg", [RPC, D])
        o_stats = p.dram_out("stats", [128, 2 * NT])

    moe_sb = p.sbuf("moe_sb", [128, KD * RPC], BF16)
    y2 = p.sbuf("y2", [128, NT * D])
    xs = [p.sbuf(f"xs{j}", [128, D]) for j in range(2)]
    identb_sb = p.sbuf("identb_sb", [128, 128], BF16)
    ln = LNBufs(p, "e3")
    tpb = [p.psum(f"tpb{j}", [128, 1024], BF16) for j in range(2)]

    s_in = p.sem("s_in")
    s_tb = p.sem("s_tb")
    s_tr = p.sem("s_tr")
    s_y = p.sem("s_y")
    s_ln = p.sem("s_ln")
    s_add = p.sem("s_add")
    s_st = p.sem("s_st")

    p.dma("sync", moe_sb.ap().rearrange("p (k t) -> p k t", t=RPC),
          moe_in.rearrange("(k p) t -> p k t", p=128), inc=s_in)
    p.dma("sync", identb_sb[:], identb[:], inc=s_in)
    ldv = p.val(s_in)

    def t_src(i, k):
        return moe_sb[:, k * RPC + i * 128:k * RPC + (i + 1) * 128]

    def t_dst(i, k0):
        return y2.ap().rearrange("p (t d) -> p t d", d=D)[
            :, i, k0 * 128:(k0 + 4) * 128].rearrange("p (k t) -> p k t", t=128)

    emit_transpose(p, t_src, t_dst, KD, NT, identb_sb, tpb, s_tb, s_tr,
                   lambda i: [(s_in, ldv)])
    trv = p.val(s_tr)
    for i in range(NT):
        xb = xs[i % 2]
        w = [] if i < 2 else [(s_add, i - 1)]
        p.dma("sync", xb[:], y1_in.rearrange("(t p) d -> p t d", p=128)[:, i, :],
              wait=w, inc=s_y)
        p.op("vector", lambda e, i=i, xb=xb: e.tensor_add(
            y2[:, i * D:(i + 1) * D], y2[:, i * D:(i + 1) * D], xb[:]),
            wait=[(s_y, p.val(s_y)), (s_tr, trv)], inc=s_add)
    yv = p.val(s_y)
    addv = p.val(s_add)
    if debug:
        p.dma("sync", o_y2.rearrange("(t p) d -> p t d", p=128),
              y2.ap().rearrange("p (t d) -> p t d", d=D),
              wait=[(s_add, addv)], inc=s_st)
    yt = lambda i: y2[:, i * D:(i + 1) * D]
    dbg_wait = [(s_st, p.val(s_st))] if debug else []
    emit_ln(p, ln, yt, yt, NT, lambda i: [(s_add, addv)] + dbg_wait, s_ln)
    if debug:
        p.dma("sync", o_stats[:, 0:NT], ln.mean[:, 0:NT],
              wait=[(s_ln, NT)], inc=s_st)
        p.dma("sync", o_stats[:, NT:2 * NT], ln.rstd[:, 0:NT],
              wait=[(s_ln, NT)], inc=s_st)
    for i in range(NT):
        xb = xs[i % 2]
        w = [(s_ln, NT), (s_add, addv)] if i < 2 else [(s_add, p.val(s_add))]
        p.dma("sync", xb[:], x1_in.rearrange("(t p) d -> p t d", p=128)[:, i, :],
              wait=w, inc=s_y)
        p.op("vector", lambda e, i=i, xb=xb: e.tensor_add(
            yt(i), yt(i), xb[:]),
            wait=[(s_y, p.val(s_y)), (s_ln, i + 1)], inc=s_add)
        p.dma("sync", o_x2.rearrange("(t p) d -> p t d", p=128)[:, i, :],
              yt(i), wait=[(s_add, p.val(s_add))], inc=s_st)
    return p


RT = 3072
NTF = RT // 128
OV = RT - RPC


def _gb_bcast(p, grow, gbc_sb, onesc, banks, s_tb, s_aux, s_ack, ldv, s_in):
    p.op("vector", lambda e: e.memset(onesc[:], 1.0), wait=[(s_in, ldv)],
         inc=s_aux)
    p.op("scalar", lambda e: e.nop(), wait=[(s_aux, p.val(s_aux))], inc=s_ack)
    for j in range(4):
        p.op("tensor", lambda e, j=j: e.matmul(
            banks[j % 2][:], onesc[:], grow[0:1, j * 512:(j + 1) * 512],
            start=True, stop=True),
            wait=[(s_ack, p.val(s_ack)), (s_in, ldv)] if j < 2 else
                 [(s_aux, p.val(s_aux))],
            inc=s_tb)
        tv = p.val(s_tb)
        p.op("vector", lambda e, j=j: e.tensor_copy(
            out=gbc_sb[:, j * 512:(j + 1) * 512], in_=banks[j % 2][:]),
            wait=[(s_tb, tv)], inc=s_aux)
    return p.val(s_aux)


def build_launch_F1():
    p = Prog()
    x_in = p.dram_in("x2pad", [RT, D])
    g_in = p.dram_in("ln1g", [1, D])
    b_in = p.dram_in("ln1b", [1, D])
    identf = p.dram_in("identf", [128, 128])
    o_y0 = p.dram_out("y0b", [RT, D])
    o_xn = p.dram_out("xn_fm", [D, RT], BF16)

    x_sb = p.sbuf("x_sb", [128, NTF * D])
    xn_fm = p.sbuf("xn_fm_sb", [128, KD * RT], BF16)
    gbc_sb = p.sbuf("gbc_sb", [128, 2 * D])
    grow = p.sbuf("grow", [1, 2 * D])
    onesc = p.sbuf("onesc", [1, 128])
    identf_sb = p.sbuf("identf_sb", [128, 128])
    ln = LNBufs(p, "f1", nt=NTF)
    lnb = LNBufs(p, "f1b", nt=NTF)
    tpf = [p.psum(f"tpf{j}", [128, 512]) for j in range(2)]

    s_in = p.sem("s_in")
    s_ln = p.sem("s_ln")
    s_ln2 = p.sem("s_ln2")
    s_tb = p.sem("s_tb")
    s_tr = p.sem("s_tr")
    s_st = p.sem("s_st")
    s_aux = p.sem("s_aux")
    s_ack = p.sem("s_ack")

    p.dma("sync", x_sb.ap().rearrange("p (t d) -> p t d", d=D),
          x_in.rearrange("(t p) d -> p t d", p=128), inc=s_in)
    p.dma("sync", identf_sb[:], identf[:], inc=s_in)
    p.dma("sync", grow[0:1, 0:D], g_in[:], inc=s_in)
    p.dma("sync", grow[0:1, D:2 * D], b_in[:], inc=s_in)
    ldv = p.val(s_in)
    gbv = _gb_bcast(p, grow, gbc_sb, onesc, tpf, s_tb, s_aux, s_ack, ldv, s_in)

    xt = lambda i: x_sb[:, i * D:(i + 1) * D]
    emit_ln(p, ln, xt, xt, NTF, lambda i: [(s_in, ldv)], s_ln)
    for i in range(NTF):
        p.dma("sync", o_y0.rearrange("(t p) d -> p t d", p=128)[:, i, :],
              xt(i), wait=[(s_ln, i + 1)], inc=s_st)
    y0st = p.val(s_st)
    emit_ln(p, lnb, xt, xt, NTF, lambda i: [(s_st, y0st), (s_aux, gbv)],
            s_ln2, gamma=gbc_sb[:, 0:D], beta=gbc_sb[:, D:2 * D])

    def t_src(i, k):
        return x_sb[:, i * D + k * 128:i * D + (k + 1) * 128]

    def t_dst(i, k0):
        v = xn_fm.ap().rearrange("p (k t) -> p k t", t=RT)
        return v[:, k0:k0 + 4, i * 128:(i + 1) * 128]

    emit_transpose(p, t_src, t_dst, KD, NTF, identf_sb, tpf, s_tb, s_tr,
                   lambda i: [(s_ln2, i + 1), (s_aux, gbv)])
    p.dma("sync", o_xn.rearrange("(k p) t -> p k t", p=128),
          xn_fm.ap().rearrange("p (k t) -> p k t", t=RT),
          wait=[(s_tr, p.val(s_tr))], inc=s_st)
    return p


def build_launch_F2():
    p = Prog()
    xn_in = p.dram_in("xn_fm", [D, RT], BF16)
    w_in = {n: p.dram_in(f"W{n}", [D, D]) for n in "kvr"}
    mix_in = p.dram_in("mixes", [128, 3 * KD])
    m0_in = p.dram_in("m0vec", [128, 1])
    o_p = {n: p.dram_out(f"{n}_fm", [D, RT]) for n in "kvr"}

    RT1 = RT + 1
    xn_sb = p.sbuf("xn_sb", [128, KD * RT1], BF16)
    mixed = p.sbuf("mixed", [128, KD * RT], BF16)
    w_sb = p.sbuf("w_sb", [128, KD * D], BF16)
    stage = [p.sbuf(f"stage{j}", [128, RT]) for j in range(2)]
    mix_sb = p.sbuf("mix_sb", [128, 3 * KD])
    m0_sb = p.sbuf("m0_sb", [128, 1])
    c1 = p.sbuf("c1", [128, 1])
    c2 = p.sbuf("c2", [128, 1])

    pj = [p.psum(f"pj{j}", [128, 512]) for j in range(4)]

    s_in = p.sem("s_in")
    s_w = p.sem("s_w")
    s_mx = p.sem("s_mx")
    s_pj = p.sem("s_pj")
    s_ev = p.sem("s_ev")
    s_st = p.sem("s_st")
    s_aux = p.sem("s_aux")
    s_ack = p.sem("s_ack")

    p.dma("sync", xn_sb.ap().rearrange("p (k t) -> p k t", t=RT1)[:, :, 1:RT1],
          xn_in.rearrange("(k p) t -> p k t", p=128), inc=s_in)
    p.dma("sync", mix_sb[:], mix_in[:], inc=s_in)
    p.dma("sync", m0_sb[:], m0_in[:], inc=s_in)
    ldv = p.val(s_in)
    for k in range(KD):
        p.op("vector", lambda e, k=k: e.memset(xn_sb[:, k * RT1:k * RT1 + 1],
                                               0.0),
             wait=([(s_in, ldv)] if k == 0 else ()),
             inc=(s_mx if k == KD - 1 else None))
    guard_v = p.val(s_mx)
    p.op("scalar", lambda e: e.nop(), wait=[(s_mx, guard_v)], inc=s_ack)
    guard_ack = p.val(s_ack)

    for wi, n in enumerate("kvr"):
        wwait = [] if wi == 0 else [(s_pj, p.val(s_pj))]
        p.dma("gpsimd", w_sb.ap().rearrange("p (k o) -> p k o", o=D),
              w_in[n].rearrange("(k p) o -> p k o", p=128), wait=wwait,
              inc=s_w)
        wv = p.val(s_w)
        mixcol = mix_sb[:, wi * KD:(wi + 1) * KD]
        for k in range(KD):
            xnv = xn_sb[:, k * RT1 + 1:k * RT1 + 1 + RT]
            xpv = xn_sb[:, k * RT1:k * RT1 + RT]
            mx = mixed[:, k * RT:(k + 1) * RT]
            wm = [(s_ack, guard_ack)]
            if wi > 0:
                wm.append((s_pj, p.val(s_pj)))
            p.op("vector", lambda e, xnv=xnv, xpv=xpv, mx=mx: e.tensor_sub(
                mx, xnv, xpv), wait=wm)
            p.op("vector", lambda e, k=k, mx=mx, xpv=xpv, mixcol=mixcol:
                 e.scalar_tensor_tensor(mx, mx, mixcol[:, k:k + 1], xpv,
                                        ALU.mult, ALU.add), inc=s_mx)
            p.op("vector", lambda e, k=k: e.tensor_scalar(
                c1[:], xn_sb[:, k * RT1 + OV:k * RT1 + OV + 1], m0_sb[:],
                None, ALU.mult), inc=s_mx)
            p.op("scalar", lambda e: e.nop(), wait=[(s_mx, p.val(s_mx))],
                 inc=s_ack)
            p.op("vector", lambda e, k=k: e.tensor_sub(
                c2[:], xn_sb[:, k * RT1 + 1 + OV:k * RT1 + 2 + OV], c1[:]),
                wait=[(s_ack, p.val(s_ack))], inc=s_mx)
            p.op("scalar", lambda e: e.nop(), wait=[(s_mx, p.val(s_mx))],
                 inc=s_ack)
            p.op("vector", lambda e, k=k, mx=mx, mixcol=mixcol:
                 e.scalar_tensor_tensor(mx[:, OV:OV + 1], c2[:],
                                        mixcol[:, k:k + 1], c1[:],
                                        ALU.mult, ALU.add),
                 wait=[(s_ack, p.val(s_ack))], inc=s_mx)
        p.op("scalar", lambda e: e.nop(), wait=[(s_mx, p.val(s_mx))],
             inc=s_ack)
        mix_ack = p.val(s_ack)
        for oc in range(KD):
            sb = stage[oc % 2]
            for tt in range(RT // 512):
                bank = pj[(oc * (RT // 512) + tt) % 4]
                w0 = [(s_w, wv), (s_ack, mix_ack)]
                if oc * (RT // 512) + tt >= 4 or wi > 0:
                    w0.append((s_ev, max(0, p.val(s_ev) - 3)))
                if oc >= 2:
                    w0.append((s_st, 16 * ((wi * KD + oc) - 1)))
                for k in range(KD):
                    p.op("tensor", lambda e, oc=oc, tt=tt, k=k, bank=bank:
                         e.matmul(bank[:],
                                  w_sb[:, k * D + oc * 128:k * D + (oc + 1) * 128],
                                  mixed[:, k * RT + tt * 512:k * RT + (tt + 1) * 512],
                                  start=(k == 0), stop=(k == KD - 1)),
                         wait=(w0 if k == 0 else ()),
                         inc=(s_pj if k == KD - 1 else None))
                pv = p.val(s_pj)
                p.op("vector", lambda e, tt=tt, sb=sb, bank=bank: e.tensor_copy(
                    out=sb[:, tt * 512:(tt + 1) * 512], in_=bank[:]),
                    wait=[(s_pj, pv)], inc=s_ev)
            p.dma("sync", o_p[n][oc * 128:(oc + 1) * 128, :], sb[:],
                  wait=[(s_ev, p.val(s_ev))], inc=s_st)
    return p


def build_launch_F3():
    p = Prog()
    k_in = p.dram_in("k_fm", [D, RT])
    v_in = p.dram_in("v_fm", [D, RT])
    r_in = p.dram_in("r_fm", [D, RT])
    td_in = p.dram_in("tdvec", [128, KD])
    tf_in = p.dram_in("tfvec", [128, KD])
    o_rw = p.dram_out("rw_fm", [D, RT], BF16)

    kc_sb = [p.sbuf(f"kc{j}", [128, RT]) for j in range(2)]
    vc_sb = [p.sbuf(f"vc{j}", [128, RT]) for j in range(2)]
    rc_sb = [p.sbuf(f"rc{j}", [128, RT]) for j in range(2)]
    ek_sb = p.sbuf("ek_sb", [128, RT])
    u_sb = p.sbuf("u_sb", [128, RT])
    a_sb = p.sbuf("a_sb", [128, RT])
    b_sb = p.sbuf("b_sb", [128, RT])
    ew_bc = p.sbuf("ew_bc", [128, RT])
    ones_rt = p.sbuf("ones_rt", [128, RT])
    rw_sb = [p.sbuf(f"rw{j}", [128, RT], BF16) for j in range(2)]
    td_sb = p.sbuf("td_sb", [128, KD])
    tf_sb = p.sbuf("tf_sb", [128, KD])
    Ev = p.sbuf("Ev", [128, 1])
    ewv = p.sbuf("ewv", [128, 1])
    etf = p.sbuf("etf", [128, 1])
    c1 = p.sbuf("c1f3", [128, 1])

    s_in = p.sem("s_in")
    s_aux = p.sem("s_aux")
    s_ack = p.sem("s_ack")
    s_st = p.sem("s_st")

    p.dma("sync", td_sb[:], td_in[:], inc=s_in)
    p.dma("sync", tf_sb[:], tf_in[:], inc=s_in)
    cst = p.val(s_in)
    p.op("vector", lambda e: e.memset(ones_rt[:], 1.0), wait=[(s_in, cst)],
         inc=s_aux)

    for kc in range(KD):
        buf = kc % 2
        wld = [] if kc < 2 else [(s_ack, p.val(s_ack))]
        p.dma("sync", kc_sb[buf][:], k_in[kc * 128:(kc + 1) * 128, :],
              wait=wld, inc=s_in)
        p.dma("sync", vc_sb[buf][:], v_in[kc * 128:(kc + 1) * 128, :],
              inc=s_in)
        p.dma("sync", rc_sb[buf][:], r_in[kc * 128:(kc + 1) * 128, :],
              inc=s_in)
        ldv = p.val(s_in)
        p.op("scalar", lambda e, kc=kc: e.activation(
            Ev[:], td_sb[:, kc:kc + 1], AF.Exp), wait=[(s_in, cst),
                                                       (s_ack, p.val(s_ack))])
        p.op("scalar", lambda e, kc=kc: e.activation(
            etf[:], tf_sb[:, kc:kc + 1], AF.Exp), inc=s_ack)
        p.op("vector", lambda e: e.engine_nop(), wait=[(s_ack, p.val(s_ack))],
             inc=s_aux)
        p.op("scalar", lambda e: e.activation(ewv[:], Ev[:], AF.Exp,
                                              scale=-1.0),
             wait=[(s_aux, p.val(s_aux))], inc=s_ack)
        p.op("vector", lambda e: e.engine_nop(), wait=[(s_ack, p.val(s_ack))],
             inc=s_aux)
        cac = p.val(s_aux)
        p.op("scalar", lambda e, buf=buf: e.activation(
            ek_sb[:], kc_sb[buf][:], AF.Exp), wait=[(s_in, ldv)], inc=s_ack)
        p.op("vector", lambda e: e.tensor_scalar(
            c1[:], ek_sb[:, OV:OV + 1], etf[:], None, ALU.mult),
            wait=[(s_ack, p.val(s_ack)), (s_aux, cac)], inc=s_aux)
        p.op("scalar", lambda e: e.nop(), wait=[(s_aux, p.val(s_aux))],
             inc=s_ack)
        p.op("vector", lambda e: e.tensor_copy(out=ek_sb[:, OV:OV + 1],
                                               in_=c1[:]),
             wait=[(s_ack, p.val(s_ack))], inc=s_aux)
        p.op("scalar", lambda e: e.nop(), wait=[(s_aux, p.val(s_aux))],
             inc=s_ack)
        ekac = p.val(s_ack)
        p.op("vector", lambda e: e.tensor_scalar(
            ew_bc[:], ones_rt[:], ewv[:], None, ALU.mult),
            wait=[(s_ack, ekac)])
        p.op("vector", lambda e, buf=buf: e.tensor_mul(u_sb[:], ek_sb[:],
                                                       vc_sb[buf][:]))
        p.op("vector", lambda e: e.tensor_tensor_scan(
            a_sb[:], ew_bc[:], u_sb[:], 0.0, ALU.mult, ALU.add))
        p.op("vector", lambda e: e.tensor_tensor_scan(
            b_sb[:], ew_bc[:], ek_sb[:], 0.0, ALU.mult, ALU.add))
        p.op("vector", lambda e: e.tensor_scalar(
            b_sb[:], b_sb[:], 1e-8, None, ALU.add))
        p.op("vector", lambda e: e.reciprocal(b_sb[:], b_sb[:]), inc=s_aux)
        p.op("scalar", lambda e, buf=buf: e.activation(
            ek_sb[:], rc_sb[buf][:], AF.Sigmoid),
            wait=[(s_aux, p.val(s_aux))], inc=s_ack)
        p.op("vector", lambda e: e.tensor_mul(a_sb[:], a_sb[:], b_sb[:]),
             wait=[(s_ack, p.val(s_ack))])
        wrw = []
        if kc >= 2:
            wrw.append((s_st, 16 * (kc - 1)))
        p.op("vector", lambda e, buf=buf: e.tensor_mul(
            rw_sb[buf][:], a_sb[:], ek_sb[:]), wait=wrw, inc=s_aux)
        p.op("scalar", lambda e: e.nop(), wait=[(s_aux, p.val(s_aux))],
             inc=s_ack)
        p.dma("sync", o_rw[kc * 128:(kc + 1) * 128, :], rw_sb[buf][:],
              wait=[(s_ack, p.val(s_ack))], inc=s_st)
    return p


def build_launch_F4():
    p = Prog()
    rw_in = p.dram_in("rw_fm", [D, RT], BF16)
    y0_in = p.dram_in("y0b", [RT, D])
    wo_in = p.dram_in("Wo", [D, D])
    g_in = p.dram_in("ln2g", [1, D])
    b_in = p.dram_in("ln2b", [1, D])
    identf = p.dram_in("identf", [128, 128])
    o_y1 = p.dram_out("y1i", [RT, D])
    o_yn = p.dram_out("yn_fm", [D, RT], BF16)

    rw_sb = p.sbuf("rw_sb", [128, KD * RT], BF16)
    w_sb = p.sbuf("w_sb", [128, KD * D], BF16)
    stage = p.sbuf("stage", [128, KD * 512])
    wt = [p.sbuf(f"wt{j}", [128, D]) for j in range(2)]
    ys = [p.sbuf(f"ys{j}", [128, D]) for j in range(2)]
    yn_fm = p.sbuf("yn_fm_sb", [128, KD * RT], BF16)
    gbc_sb = p.sbuf("gbc_sb", [128, 2 * D])
    grow = p.sbuf("grow", [1, 2 * D])
    onesc = p.sbuf("onesc", [1, 128])
    identf_sb = p.sbuf("identf_sb", [128, 128])
    ln = LNBufs(p, "f4", nt=NTF)
    pj = [p.psum(f"pj{j}", [128, 512]) for j in range(2)]
    tpf = [p.psum(f"tpf{j}", [128, 512]) for j in range(2)]

    s_in = p.sem("s_in")
    s_pj = p.sem("s_pj")
    s_ev = p.sem("s_ev")
    s_tb = p.sem("s_tb")
    s_tr = p.sem("s_tr")
    s_y = p.sem("s_y")
    s_add1 = p.sem("s_add1")
    s_sq1 = p.sem("s_sq1")
    s_st = p.sem("s_st")
    s_aux = p.sem("s_aux")
    s_ack = p.sem("s_ack")
    s_g = p.sem("s_g")

    p.dma("sync", rw_sb.ap().rearrange("p (k t) -> p k t", t=RT),
          rw_in.rearrange("(k p) t -> p k t", p=128), inc=s_in)
    p.dma("sync", identf_sb[:], identf[:], inc=s_in)
    p.dma("sync", grow[0:1, 0:D], g_in[:], inc=s_in)
    p.dma("sync", grow[0:1, D:2 * D], b_in[:], inc=s_in)
    p.dma("gpsimd", w_sb.ap().rearrange("p (k o) -> p k o", o=D),
          wo_in.rearrange("(k p) o -> p k o", p=128), inc=s_in)
    ldv = p.val(s_in)
    gbv = _gb_bcast(p, grow, gbc_sb, onesc, tpf, s_tb, s_aux, s_ack, ldv, s_in)
    inv_d = 1.0 / D
    ln.eps_done = True
    p.op("vector", lambda e: e.memset(ln.epsb[:], EPS))

    for tt in range(RT // 512):
        for oc in range(KD):
            bank = pj[oc % 2]
            w = [(s_in, ldv), (s_aux, gbv)]
            if tt * KD + oc >= 2:
                w.append((s_ev, p.val(s_ev) - 1))
            for k in range(KD):
                p.op("tensor", lambda e, oc=oc, tt=tt, k=k, bank=bank:
                     e.matmul(bank[:],
                              w_sb[:, k * D + oc * 128:k * D + (oc + 1) * 128],
                              rw_sb[:, k * RT + tt * 512:k * RT + (tt + 1) * 512],
                              start=(k == 0), stop=(k == KD - 1)),
                     wait=(w if k == 0 else ()),
                     inc=(s_pj if k == KD - 1 else None))
            pv = p.val(s_pj)
            p.op("vector", lambda e, oc=oc, bank=bank: e.tensor_copy(
                out=stage[:, oc * 512:(oc + 1) * 512], in_=bank[:]),
                wait=[(s_pj, pv)], inc=s_ev)
        evv = p.val(s_ev)
        for i4 in range(4):
            i = tt * 4 + i4
            w_tile = wt[i % 2]
            yb = ys[i % 2]
            for k0 in (0, 4):
                bank2 = tpf[k0 // 4]
                wv = [(s_ev, evv)]
                if p.val(s_tr) >= 2:
                    wv.append((s_tr, p.val(s_tr) - 1))
                if i >= 2 and k0 == 0:
                    wv.append((s_sq1, i - 1))
                    wv.append((s_st, 16 * (i - 1)))
                for j in range(4):
                    p.op("tensor", lambda e, i4=i4, j=j, k0=k0, bank2=bank2:
                         e.transpose(bank2[:, j * 128:(j + 1) * 128],
                                     stage[:, (k0 + j) * 512 + i4 * 128:
                                           (k0 + j) * 512 + (i4 + 1) * 128],
                                     identf_sb[:]),
                         wait=(wv if j == 0 else ()),
                         inc=(s_tb if j == 3 else None))
                tbv = p.val(s_tb)
                p.op("vector", lambda e, k0=k0, w_tile=w_tile, bank2=bank2:
                     e.tensor_copy(out=w_tile[:, k0 * 128:(k0 + 4) * 128],
                                   in_=bank2[:]),
                     wait=[(s_tb, tbv)], inc=s_tr)
            trv = p.val(s_tr)
            wy = [] if i < 2 else [(s_add1, i - 1)]
            p.dma("sync", yb[:],
                  y0_in.rearrange("(t p) d -> p t d", p=128)[:, i, :],
                  wait=wy, inc=s_y)
            p.op("vector", lambda e, w_tile=w_tile, yb=yb: e.tensor_add(
                w_tile[:], w_tile[:], yb[:]),
                wait=[(s_y, p.val(s_y)), (s_tr, trv)], inc=s_add1)
            av = p.val(s_add1)
            p.dma("sync", o_y1.rearrange("(t p) d -> p t d", p=128)[:, i, :],
                  w_tile[:], wait=[(s_add1, av)], inc=s_st)
            p.op("vector", lambda e, i=i, w_tile=w_tile: e.reduce_sum(
                ln.msum[:, i:i + 1], w_tile[:], AX.X))
            p.op("scalar", lambda e, i=i, w_tile=w_tile: e.activation(
                ln.sc[:], w_tile[:], AF.Square,
                accum_out=ln.sqs[:, i:i + 1]),
                wait=[(s_add1, av)], inc=s_sq1)
    p.op("vector", lambda e: e.engine_nop(), wait=[(s_sq1, NTF)], inc=s_aux)
    p.op("scalar", lambda e: e.nop(), wait=[(s_aux, p.val(s_aux))], inc=s_ack)
    p.op("vector", lambda e: e.tensor_scalar_mul(
        ln.mean[:, 0:NTF], ln.msum[:, 0:NTF], inv_d),
        wait=[(s_ack, p.val(s_ack))], inc=s_aux)
    p.op("scalar", lambda e: e.nop(), wait=[(s_aux, p.val(s_aux))], inc=s_ack)
    p.op("vector", lambda e: e.tensor_mul(
        ln.m2[:, 0:NTF], ln.mean[:, 0:NTF], ln.mean[:, 0:NTF]),
        wait=[(s_ack, p.val(s_ack))], inc=s_aux)
    p.op("scalar", lambda e: e.nop(), wait=[(s_aux, p.val(s_aux))], inc=s_ack)
    p.op("vector", lambda e: e.scalar_tensor_tensor(
        ln.vr[:, 0:NTF], ln.sqs[:, 0:NTF], inv_d, ln.m2[:, 0:NTF],
        ALU.mult, ALU.subtract), wait=[(s_ack, p.val(s_ack))], inc=s_aux)
    p.op("scalar", lambda e: e.activation(
        ln.vln[:, 0:NTF], ln.vr[:, 0:NTF], AF.Ln, bias=ln.epsb[:]),
        wait=[(s_aux, p.val(s_aux))], inc=s_ack)
    p.op("vector", lambda e: e.engine_nop(), wait=[(s_ack, p.val(s_ack))],
         inc=s_aux)
    p.op("scalar", lambda e: e.activation(
        ln.rstd[:, 0:NTF], ln.vln[:, 0:NTF], AF.Exp, scale=-0.5),
        wait=[(s_aux, p.val(s_aux))], inc=s_ack)
    rstd_ack = p.val(s_ack)

    for i in range(NTF):
        yb = ys[i % 2]
        w_tile = wt[i % 2]
        wy = [(s_st, 16 * NTF)] if i < 2 else [(s_g, p.val(s_g))]
        p.dma("sync", yb[:], o_y1.rearrange("(t p) d -> p t d", p=128)[:, i, :],
              wait=wy, inc=s_y)
        p.op("vector", lambda e, i=i, yb=yb, w_tile=w_tile: e.tensor_scalar(
            w_tile[:], yb[:], ln.mean[:, i:i + 1], ln.rstd[:, i:i + 1],
            ALU.subtract, ALU.mult),
            wait=[(s_y, p.val(s_y)), (s_ack, rstd_ack)])
        p.op("vector", lambda e, w_tile=w_tile: e.tensor_mul(
            w_tile[:], w_tile[:], gbc_sb[:, 0:D]))
        p.op("vector", lambda e, w_tile=w_tile: e.tensor_add(
            w_tile[:], w_tile[:], gbc_sb[:, D:2 * D]), inc=s_aux)
        zv = p.val(s_aux)
        for k0 in (0, 4):
            bank2 = tpf[k0 // 4]
            wv = [(s_aux, zv)]
            if p.val(s_tr) >= 2:
                wv.append((s_tr, p.val(s_tr) - 1))
            for j in range(4):
                p.op("tensor", lambda e, j=j, k0=k0, w_tile=w_tile, bank2=bank2:
                     e.transpose(bank2[:, j * 128:(j + 1) * 128],
                                 w_tile[:, (k0 + j) * 128:(k0 + j + 1) * 128],
                                 identf_sb[:]),
                     wait=(wv if j == 0 else ()),
                     inc=(s_tb if j == 3 else None))
            tbv = p.val(s_tb)
            yv_dst = yn_fm.ap().rearrange("p (k t) -> p k t", t=RT)
            p.op("vector", lambda e, i=i, k0=k0, bank2=bank2, yv_dst=yv_dst:
                 e.tensor_copy(out=yv_dst[:, k0:k0 + 4, i * 128:(i + 1) * 128],
                               in_=bank2.ap().rearrange("p (k t) -> p k t",
                                                        t=128)),
                 wait=[(s_tb, tbv)], inc=s_tr)
        p.op("vector", lambda e: e.engine_nop(), inc=s_g)
    p.dma("sync", o_yn.rearrange("(k p) t -> p k t", p=128),
          yn_fm.ap().rearrange("p (k t) -> p k t", t=RT),
          wait=[(s_tr, p.val(s_tr))], inc=s_st)
    return p


def build_launch_F5():
    p = Prog()
    yn_in = p.dram_in("yn_fm", [D, RT], BF16)
    y1_in = p.dram_in("y1i", [RT, D])
    x2_in = p.dram_in("x2", [RPC, D])
    wfk_in = p.dram_in("Wfk", [D, 4096])
    wfv_in = p.dram_in("Wfv", [4096, D])
    wfr_in = p.dram_in("Wfr", [D, D])
    identf = p.dram_in("identf", [128, 128])
    o_inner = p.dram_out("inner", [RPC, D])
    o_x3 = p.dram_out("x3", [RPC, D])

    HID = 4096
    NHC = HID // 128
    W = RPC + 1
    yn_sb = p.sbuf("yn_sb", [128, KD * W], BF16)
    z_sb = p.sbuf("z_sb", [128, KD * 512], BF16)
    wfk_sb = p.sbuf("wfk_sb", [128, KD * HID], BF16)
    wfv_sb = [p.sbuf(f"wfv{j}", [128, NHC * 128], BF16) for j in range(2)]
    wfr_sb = p.sbuf("wfr_sb", [128, KD * D], BF16)
    kk_sb = p.sbuf("kk_sb", [128, NHC * 512], BF16)
    rr_sb = p.sbuf("rr_sb", [128, KD * 512], BF16)
    stage = p.sbuf("stage", [128, KD * 512], BF16)
    wt = [p.sbuf(f"wt{j}", [128, D]) for j in range(2)]
    ys = [p.sbuf(f"ys{j}", [128, D]) for j in range(2)]
    identf_sb = p.sbuf("identf_sb", [128, 128])
    identb_sb = p.sbuf("identb_sb", [128, 128], BF16)
    ln = LNBufs(p, "f5", nt=NT)
    pj = [p.psum(f"pj{j}", [128, 512]) for j in range(2)]
    po = [p.psum(f"po{j}", [128, 512]) for j in range(2)]
    tpf = [p.psum(f"tpf{j}", [128, 1024], BF16) for j in range(2)]

    s_in = p.sem("s_in")
    s_w = p.sem("s_w")
    s_pj = p.sem("s_pj")
    s_h = p.sem("s_h")
    s_ev = p.sem("s_ev")
    s_tb = p.sem("s_tb")
    s_tr = p.sem("s_tr")
    s_y = p.sem("s_y")
    s_add1 = p.sem("s_add1")
    s_sq1 = p.sem("s_sq1")
    s_st = p.sem("s_st")
    s_aux = p.sem("s_aux")
    s_ack = p.sem("s_ack")
    s_g = p.sem("s_g")
    s_ze = p.sem("s_ze")

    p.dma("sync", yn_sb.ap().rearrange("p (k t) -> p k t", t=W),
          yn_in.rearrange("(k p) t -> p k t", p=128)[:, :, OV - 1:RT],
          inc=s_in)
    p.dma("sync", identf_sb[:], identf[:], inc=s_in)
    p.dma("gpsimd", wfk_sb.ap().rearrange("p (k o) -> p k o", o=HID),
          wfk_in.rearrange("(k p) o -> p k o", p=128), inc=s_in)
    p.dma("gpsimd", wfr_sb.ap().rearrange("p (k o) -> p k o", o=D),
          wfr_in.rearrange("(k p) o -> p k o", p=128), inc=s_in)
    ldv = p.val(s_in)
    inv_d = 1.0 / D
    ln.eps_done = True
    p.op("vector", lambda e: e.memset(ln.epsb[:], EPS))
    p.op("vector", lambda e: e.tensor_copy(out=identb_sb[:], in_=identf_sb[:]),
         wait=[(s_in, ldv)], inc=s_ze)

    wfv_last = [0, 0]

    for tt in range(4):
        for k in range(KD):
            ynv = yn_sb[:, k * W + 1 + tt * 512:k * W + 1 + (tt + 1) * 512]
            ypv = yn_sb[:, k * W + tt * 512:k * W + (tt + 1) * 512]
            wz = [(s_in, ldv)]
            if tt > 0 and k == 0:
                wz.append((s_h, p.val(s_h)))
            p.op("vector", lambda e, k=k, ynv=ynv, ypv=ypv: e.tensor_add(
                z_sb[:, k * 512:(k + 1) * 512], ynv, ypv), wait=wz)
            p.op("vector", lambda e, k=k: e.tensor_scalar_mul(
                z_sb[:, k * 512:(k + 1) * 512],
                z_sb[:, k * 512:(k + 1) * 512], 0.5),
                inc=(s_ze if k == KD - 1 else None))
        zev = p.val(s_ze)
        for oc in range(KD):
            bank = pj[oc % 2]
            w0 = [(s_in, ldv), (s_ze, zev)]
            if oc >= 2:
                w0.append((s_g, p.val(s_g) - 1))
            for k in range(KD):
                ynv = yn_sb[:, k * W + 1 + tt * 512:k * W + 1 + (tt + 1) * 512]
                p.op("tensor", lambda e, oc=oc, k=k, bank=bank, ynv=ynv:
                     e.matmul(bank[:],
                              wfr_sb[:, k * D + oc * 128:k * D + (oc + 1) * 128],
                              ynv, start=(k == 0), stop=(k == KD - 1)),
                     wait=(w0 if k == 0 else ()),
                     inc=(s_pj if k == KD - 1 else None))
            p.op("scalar", lambda e, oc=oc, bank=bank: e.activation(
                rr_sb[:, oc * 512:(oc + 1) * 512], bank[:], AF.Sigmoid),
                wait=[(s_pj, p.val(s_pj))], inc=s_g)
        for hc in range(NHC):
            bank = pj[hc % 2]
            w0 = [(s_ze, zev)]
            if hc >= 2 or True:
                w0.append((s_g, p.val(s_g) - 1 if hc < 2 else p.val(s_g)))
            if hc >= 2:
                w0 = [(s_ze, zev), (s_h, p.val(s_h) - 1)]
            for k in range(KD):
                p.op("tensor", lambda e, hc=hc, k=k, bank=bank:
                     e.matmul(bank[:],
                              wfk_sb[:, k * HID + hc * 128:k * HID + (hc + 1) * 128],
                              z_sb[:, k * 512:(k + 1) * 512],
                              start=(k == 0), stop=(k == KD - 1)),
                     wait=(w0 if k == 0 else ()),
                     inc=(s_pj if k == KD - 1 else None))
            p.op("scalar", lambda e, hc=hc, bank=bank: e.activation(
                ln.sc[:, 0:512], bank[:], AF.Relu),
                wait=[(s_pj, p.val(s_pj))])
            p.op("scalar", lambda e, hc=hc: e.activation(
                kk_sb[:, hc * 512:(hc + 1) * 512], ln.sc[:, 0:512],
                AF.Square), inc=s_h)
        hv = p.val(s_h)
        for oc in range(KD):
            buf = oc % 2
            wld = [(s_pj, wfv_last[buf])] if wfv_last[buf] > 0 else []
            p.dma("gpsimd", wfv_sb[buf].ap().rearrange("p (k o) -> p k o",
                                                       o=128),
                  wfv_in.rearrange("(k p) o -> p k o", p=128)[:, :, oc * 128:(oc + 1) * 128],
                  wait=wld, inc=s_w)
            w2v = p.val(s_w)
            bank = po[oc % 2]
            wmm = [(s_h, hv), (s_w, w2v)]
            if p.val(s_ev) >= 2:
                wmm.append((s_ev, p.val(s_ev) - 1))
            for k in range(NHC):
                p.op("tensor", lambda e, k=k, bank=bank, buf=buf:
                     e.matmul(bank[:],
                              wfv_sb[buf][:, k * 128:(k + 1) * 128],
                              kk_sb[:, k * 512:(k + 1) * 512],
                              start=(k == 0), stop=(k == NHC - 1)),
                     wait=(wmm if k == 0 else ()),
                     inc=(s_pj if k == NHC - 1 else None))
            omv = p.val(s_pj)
            p.op("vector", lambda e, oc=oc, bank=bank: e.tensor_mul(
                stage[:, oc * 512:(oc + 1) * 512], bank[:],
                rr_sb[:, oc * 512:(oc + 1) * 512]),
                wait=[(s_pj, omv)], inc=s_ev)
            wfv_last[buf] = omv
        evv = p.val(s_ev)
        for i4 in range(4):
            i = tt * 4 + i4
            w_tile = wt[i % 2]
            yb = ys[i % 2]
            for k0 in (0, 4):
                bank2 = tpf[k0 // 4]
                wv = [(s_ev, evv)]
                if p.val(s_tr) >= 2:
                    wv.append((s_tr, p.val(s_tr) - 1))
                if i >= 2 and k0 == 0:
                    wv.append((s_sq1, i - 1))
                    wv.append((s_st, 16 * (i - 1)))
                for j in range(4):
                    p.op("tensor", lambda e, i4=i4, j=j, k0=k0, bank2=bank2:
                         e.transpose(bank2[:, j * 128:(j + 1) * 128],
                                     stage[:, (k0 + j) * 512 + i4 * 128:
                                           (k0 + j) * 512 + (i4 + 1) * 128],
                                     identb_sb[:]),
                         wait=(wv if j == 0 else ()),
                         inc=(s_tb if j == 3 else None))
                tbv = p.val(s_tb)
                p.op("vector", lambda e, k0=k0, w_tile=w_tile, bank2=bank2:
                     e.tensor_copy(out=w_tile[:, k0 * 128:(k0 + 4) * 128],
                                   in_=bank2[:, 0:512]),
                     wait=[(s_tb, tbv)], inc=s_tr)
            trv = p.val(s_tr)
            wy = [] if i < 2 else [(s_add1, i - 1)]
            p.dma("sync", yb[:],
                  y1_in.rearrange("(t p) d -> p t d", p=128)[:, (OV // 128) + i, :],
                  wait=wy, inc=s_y)
            p.op("vector", lambda e, w_tile=w_tile, yb=yb: e.tensor_add(
                w_tile[:], w_tile[:], yb[:]),
                wait=[(s_y, p.val(s_y)), (s_tr, trv)], inc=s_add1)
            av = p.val(s_add1)
            p.dma("sync", o_inner.rearrange("(t p) d -> p t d", p=128)[:, i, :],
                  w_tile[:], wait=[(s_add1, av)], inc=s_st)
            p.op("vector", lambda e, i=i, w_tile=w_tile: e.reduce_sum(
                ln.msum[:, i:i + 1], w_tile[:], AX.X))
            p.op("scalar", lambda e, i=i, w_tile=w_tile: e.activation(
                ln.sc[:, 0:D] if False else ln.sc[:], w_tile[:], AF.Square,
                accum_out=ln.sqs[:, i:i + 1]),
                wait=[(s_add1, av)], inc=s_sq1)
    p.op("vector", lambda e: e.engine_nop(), wait=[(s_sq1, NT)], inc=s_aux)
    p.op("scalar", lambda e: e.nop(), wait=[(s_aux, p.val(s_aux))], inc=s_ack)
    p.op("vector", lambda e: e.tensor_scalar_mul(
        ln.mean[:, 0:NT], ln.msum[:, 0:NT], inv_d),
        wait=[(s_ack, p.val(s_ack))], inc=s_aux)
    p.op("scalar", lambda e: e.nop(), wait=[(s_aux, p.val(s_aux))], inc=s_ack)
    p.op("vector", lambda e: e.tensor_mul(
        ln.m2[:, 0:NT], ln.mean[:, 0:NT], ln.mean[:, 0:NT]),
        wait=[(s_ack, p.val(s_ack))], inc=s_aux)
    p.op("scalar", lambda e: e.nop(), wait=[(s_aux, p.val(s_aux))], inc=s_ack)
    p.op("vector", lambda e: e.scalar_tensor_tensor(
        ln.vr[:, 0:NT], ln.sqs[:, 0:NT], inv_d, ln.m2[:, 0:NT],
        ALU.mult, ALU.subtract), wait=[(s_ack, p.val(s_ack))], inc=s_aux)
    p.op("scalar", lambda e: e.activation(
        ln.vln[:, 0:NT], ln.vr[:, 0:NT], AF.Ln, bias=ln.epsb[:]),
        wait=[(s_aux, p.val(s_aux))], inc=s_ack)
    p.op("vector", lambda e: e.engine_nop(), wait=[(s_ack, p.val(s_ack))],
         inc=s_aux)
    p.op("scalar", lambda e: e.activation(
        ln.rstd[:, 0:NT], ln.vln[:, 0:NT], AF.Exp, scale=-0.5),
        wait=[(s_aux, p.val(s_aux))], inc=s_ack)
    rstd_ack = p.val(s_ack)
    for i in range(NT):
        yb = ys[i % 2]
        w_tile = wt[i % 2]
        wy = [(s_st, 16 * NT), (s_add1, NT)] if i < 2 else [(s_g, p.val(s_g))]
        p.dma("sync", yb[:],
              o_inner.rearrange("(t p) d -> p t d", p=128)[:, i, :],
              wait=wy, inc=s_y)
        p.op("vector", lambda e, i=i, yb=yb, w_tile=w_tile: e.tensor_scalar(
            w_tile[:], yb[:], ln.mean[:, i:i + 1], ln.rstd[:, i:i + 1],
            ALU.subtract, ALU.mult),
            wait=[(s_y, p.val(s_y)), (s_ack, rstd_ack)], inc=s_g)
        p.dma("sync", yb[:], x2_in.rearrange("(t p) d -> p t d", p=128)[:, i, :],
              wait=[(s_g, p.val(s_g))], inc=s_y)
        p.op("vector", lambda e, w_tile=w_tile, yb=yb: e.tensor_add(
            w_tile[:], w_tile[:], yb[:]), wait=[(s_y, p.val(s_y))], inc=s_g)
        p.dma("sync", o_x3.rearrange("(t p) d -> p t d", p=128)[:, i, :],
              w_tile[:], wait=[(s_g, p.val(s_g))], inc=s_st)
    return p


def _emit_tern_scale(p, w_in, nk, ncols, tag, s_in, s_aux, s_ack, invb_out,
                     tpb, extra_wait=()):
    row = p.sbuf(f"tsc_row_{tag}", [1, 128])
    wld = p.sbuf(f"tsc_w_{tag}", [128, nk * 128])
    acc = p.sbuf(f"tsc_a_{tag}", [128, 2])
    tot = p.sbuf(f"tsc_t_{tag}", [1, 2])
    onesr = p.sbuf(f"tsc_o_{tag}", [128, 1])
    s_l = p.sem(f"tsc_l_{tag}")
    s_v = p.sem(f"tsc_v_{tag}")
    p.op("vector", lambda e: e.memset(acc[:, 0:1], 0.0), wait=extra_wait)
    p.op("vector", lambda e, row=row: e.memset(row[:], 1.0))
    p.op("vector", lambda e: e.memset(onesr[:], 1.0), inc=s_v)
    nblk = ncols // 128
    for b_ in range(nblk):
        p.dma("sync", wld.ap().rearrange("p (k c) -> p k c", c=128),
              w_in.rearrange("(k p) c -> p k c", p=128)[:, :, b_ * 128:(b_ + 1) * 128],
              wait=[(s_v, p.val(s_v))], inc=s_l)
        lv = p.val(s_l)
        p.op("vector", lambda e: e.tensor_reduce(
            acc[:, 1:2], wld[:], AX.X, ALU.add, apply_absolute_value=True),
            wait=[(s_l, lv)], inc=s_v)
        p.op("scalar", lambda e: e.nop(), wait=[(s_v, p.val(s_v))], inc=s_ack)
        p.op("vector", lambda e: e.tensor_add(acc[:, 0:1], acc[:, 0:1],
                                              acc[:, 1:2]),
             wait=[(s_ack, p.val(s_ack))], inc=s_v)
    p.op("scalar", lambda e: e.nop(), wait=[(s_v, p.val(s_v))], inc=s_ack)
    p.op("tensor", lambda e: e.matmul(tpb[0:1, 0:1], onesr[:], acc[:, 0:1],
                                      start=True, stop=True),
         wait=[(s_ack, p.val(s_ack))], inc=s_l, inc_n=1)
    p.op("scalar", lambda e: e.copy(tot[0:1, 0:1], tpb[0:1, 0:1]),
         wait=[(s_l, p.val(s_l))], inc=s_ack)
    p.op("vector", lambda e, n=nk * 128 * ncols: e.tensor_scalar(
        tot[0:1, 1:2], tot[0:1, 0:1], 1.0 / n, 1e-8, ALU.mult, ALU.max),
        wait=[(s_ack, p.val(s_ack))], inc=s_v)
    p.op("scalar", lambda e: e.nop(), wait=[(s_v, p.val(s_v))], inc=s_ack)
    p.op("vector", lambda e: e.reciprocal(tot[0:1, 0:1], tot[0:1, 1:2]),
         wait=[(s_ack, p.val(s_ack))], inc=s_v)
    p.op("scalar", lambda e: e.nop(), wait=[(s_v, p.val(s_v))], inc=s_ack)
    p.op("tensor", lambda e: e.matmul(tpb[0:128, 0:1], row[0:1, 0:128],
                                      tot[0:1, 0:1], start=True, stop=True),
         wait=[(s_ack, p.val(s_ack)), (s_v, p.val(s_v))],
         inc=s_l, inc_n=1)
    p.op("vector", lambda e: e.tensor_copy(out=invb_out[:], in_=tpb[0:128, 0:1]),
         wait=[(s_l, p.val(s_l))], inc=s_v)
    p.op("scalar", lambda e: e.nop(), wait=[(s_v, p.val(s_v))], inc=s_ack)
    return p.val(s_ack), row, s_l, s_v


def build_launch_F6a():
    p = Prog()
    x3_in = p.dram_in("x3", [RPC, D])
    w1T_in = p.dram_in("W1T", [D, 4096])
    b1_in = p.dram_in("b1", [1, 4096])
    identf = p.dram_in("identf", [128, 128])
    o_h = p.dram_out("h_fm", [4096, RPC], BF16)

    x_sb = p.sbuf("x_sb", [128, NT * D])
    xn_fm = p.sbuf("xn_fm_sb", [128, KD * RPC], BF16)
    wld = [p.sbuf(f"wld{j}", [128, KD * 128]) for j in range(2)]
    wt_sb = [p.sbuf(f"wtb{j}", [128, KD * 128], BF16) for j in range(2)]
    wabs = p.sbuf("wabs", [128, KD * 128])
    hst = [p.sbuf(f"hst{j}", [128, RPC], BF16) for j in range(2)]
    b1_sb = p.sbuf("b1_sb", [128, 32])
    invb = p.sbuf("invb", [128, 1])
    identf_sb = p.sbuf("identf_sb", [128, 128])
    ln = LNBufs(p, "f6a")
    pj = [p.psum(f"pj{j}", [128, 512]) for j in range(2)]
    tpf = [p.psum(f"tpf{j}", [128, 512]) for j in range(2)]
    tpx = p.psum("tpx", [128, 512])

    s_in = p.sem("s_in")
    s_ln = p.sem("s_ln")
    s_tb = p.sem("s_tb")
    s_tr = p.sem("s_tr")
    s_w = p.sem("s_w")
    s_t = p.sem("s_t")
    s_pj = p.sem("s_pj")
    s_h = p.sem("s_h")
    s_st = p.sem("s_st")
    s_aux = p.sem("s_aux")
    s_ack = p.sem("s_ack")

    p.dma("sync", x_sb.ap().rearrange("p (t d) -> p t d", d=D),
          x3_in.rearrange("(t p) d -> p t d", p=128), inc=s_in)
    p.dma("sync", identf_sb[:], identf[:], inc=s_in)
    p.dma("sync", b1_sb.ap().rearrange("p (k o) -> p k o", o=1),
          b1_in.rearrange("o (k p) -> p k o", p=128), inc=s_in,
          allow_slow_non_contiguous=True)
    ldv = p.val(s_in)

    ack0, row, s_l, s_v = _emit_tern_scale(
        p, w1T_in, KD, 4096, "w1", s_in, s_aux, s_ack, invb, tpx,
        extra_wait=[(s_in, ldv)])

    xt = lambda i: x_sb[:, i * D:(i + 1) * D]
    emit_ln(p, ln, xt, xt, NT, lambda i: [(s_in, ldv)], s_ln)

    def t_src(i, k):
        return x_sb[:, i * D + k * 128:i * D + (k + 1) * 128]

    def t_dst(i, k0):
        v = xn_fm.ap().rearrange("p (k t) -> p k t", t=RPC)
        return v[:, k0:k0 + 4, i * 128:(i + 1) * 128]

    emit_transpose(p, t_src, t_dst, KD, NT, identf_sb, tpf, s_tb, s_tr,
                   lambda i: [(s_ln, i + 1)])
    trv = p.val(s_tr)

    for oc in range(32):
        buf = oc % 2
        wl = [(s_ack, ack0)] if oc < 2 else [(s_t, p.val(s_t) - 1)]
        p.dma("gpsimd", wld[buf].ap().rearrange("p (k c) -> p k c", c=128),
              w1T_in.rearrange("(k p) c -> p k c", p=128)[:, :, oc * 128:(oc + 1) * 128],
              wait=wl, inc=s_w)
        wv = p.val(s_w)
        p.op("vector", lambda e, buf=buf: e.tensor_reduce(
            wabs[:], wld[buf][:], AX.XYZW, ALU.bypass)
            if False else e.tensor_scalar(
                wabs[:], wld[buf][:], invb[:], None, ALU.mult),
            wait=[(s_w, wv), (s_ack, ack0)], inc=s_aux)
        p.op("vector", lambda e: e.tensor_scalar(
            wabs[:], wabs[:], 0.5, None, ALU.is_ge,
            ) if False else e.engine_nop(), inc=s_aux)
        p.op("vector", lambda e: e.tensor_mul(wabs[:], wabs[:], wabs[:]),
             inc=s_aux)
        p.op("vector", lambda e: e.tensor_scalar(
            wabs[:], wabs[:], 0.25, None, ALU.is_ge), inc=s_aux)
        p.op("scalar", lambda e, buf=buf: e.activation(
            wt_sb[buf][:], wld[buf][:], AF.Sign),
            wait=[(s_aux, p.val(s_aux))], inc=s_ack)
        p.op("vector", lambda e, buf=buf: e.tensor_mul(
            wt_sb[buf][:], wt_sb[buf][:], wabs[:]),
            wait=[(s_ack, p.val(s_ack))], inc=s_t)
        tv = p.val(s_t)
        hb = hst[buf]
        for tt in range(4):
            bank = pj[tt % 2]
            w0 = [(s_t, tv), (s_tr, trv)]
            if oc * 4 + tt >= 2:
                w0.append((s_h, p.val(s_h) - 1))
            for k in range(KD):
                p.op("tensor", lambda e, k=k, tt=tt, bank=bank, buf=buf:
                     e.matmul(bank[:],
                              wt_sb[buf][:, k * 128:(k + 1) * 128],
                              xn_fm[:, k * RPC + tt * 512:k * RPC + (tt + 1) * 512],
                              start=(k == 0), stop=(k == KD - 1)),
                     wait=(w0 if k == 0 else ()),
                     inc=(s_pj if k == KD - 1 else None))
            p.op("scalar", lambda e, oc=oc, tt=tt, bank=bank, hb=hb:
                 e.activation(hb[:, tt * 512:(tt + 1) * 512], bank[:],
                              AF.Gelu, bias=b1_sb[:, oc:oc + 1]),
                 wait=[(s_pj, p.val(s_pj))], inc=s_h)
        wst = [] if oc < 2 else [(s_st, 16 * (oc - 1))]
        p.dma("sync", o_h[oc * 128:(oc + 1) * 128, :], hb[:],
              wait=[(s_h, p.val(s_h))] + wst, inc=s_st)
    return p


def build_launch_F6b():
    p = Prog()
    h_in = p.dram_in("h_fm", [4096, RPC], BF16)
    w2T_in = p.dram_in("W2T", [4096, D])
    b2_in = p.dram_in("b2", [1, D])
    x3_in = p.dram_in("x3", [RPC, D])
    identf = p.dram_in("identf", [128, 128])
    o_x4 = p.dram_out("x4", [RPC, D])
    o_f = p.dram_out("ffnout", [RPC, D])

    NHC = 32
    h_sb = p.sbuf("h_sb", [128, NHC * 512], BF16)
    wld = p.sbuf("wld", [128, NHC * 128])
    wt_full = p.sbuf("wt_full", [128, NHC * D], BF16)
    wabs = p.sbuf("wabs", [128, NHC * 128])
    stage = p.sbuf("stage", [128, KD * 512])
    wt = [p.sbuf(f"wt{j}", [128, D]) for j in range(2)]
    xs = [p.sbuf(f"xs{j}", [128, D]) for j in range(2)]
    b2_sb = p.sbuf("b2_sb", [128, KD])
    invb = p.sbuf("invb", [128, 1])
    identf_sb = p.sbuf("identf_sb", [128, 128])
    ln = LNBufs(p, "f6b")
    pj = [p.psum(f"pj{j}", [128, 512]) for j in range(2)]
    tpf = [p.psum(f"tpf{j}", [128, 512]) for j in range(2)]
    tpx = p.psum("tpx", [128, 512])

    s_in = p.sem("s_in")
    s_hl = p.sem("s_hl")
    s_w = p.sem("s_w")
    s_t = p.sem("s_t")
    s_pj = p.sem("s_pj")
    s_ev = p.sem("s_ev")
    s_tb = p.sem("s_tb")
    s_tr = p.sem("s_tr")
    s_y = p.sem("s_y")
    s_add1 = p.sem("s_add1")
    s_sq1 = p.sem("s_sq1")
    s_g = p.sem("s_g")
    s_st = p.sem("s_st")
    s_aux = p.sem("s_aux")
    s_ack = p.sem("s_ack")

    p.dma("sync", identf_sb[:], identf[:], inc=s_in)
    p.dma("sync", b2_sb.ap().rearrange("p (k o) -> p k o", o=1),
          b2_in.rearrange("o (k p) -> p k o", p=128), inc=s_in,
          allow_slow_non_contiguous=True)
    ldv = p.val(s_in)
    inv_d = 1.0 / D
    ln.eps_done = True
    p.op("vector", lambda e: e.memset(ln.epsb[:], EPS))

    ack0, row, s_l, s_v = _emit_tern_scale(
        p, w2T_in, NHC, D, "w2", s_in, s_aux, s_ack, invb, tpx,
        extra_wait=[(s_in, ldv)])

    for oc in range(KD):
        wl = [(s_ack, ack0)] if oc == 0 else [(s_t, p.val(s_t))]
        p.dma("gpsimd", wld.ap().rearrange("p (k c) -> p k c", c=128),
              w2T_in.rearrange("(k p) c -> p k c", p=128)[:, :, oc * 128:(oc + 1) * 128],
              wait=wl, inc=s_w)
        wv = p.val(s_w)
        p.op("vector", lambda e: e.tensor_scalar(
            wabs[:], wld[:], invb[:], None, ALU.mult),
            wait=[(s_w, wv), (s_ack, ack0)], inc=s_aux)
        p.op("vector", lambda e: e.tensor_mul(wabs[:], wabs[:], wabs[:]))
        p.op("vector", lambda e: e.tensor_scalar(
            wabs[:], wabs[:], 0.25, None, ALU.is_ge), inc=s_aux)
        seg = wt_full[:, oc * NHC * 128:(oc + 1) * NHC * 128]
        p.op("scalar", lambda e, seg=seg: e.activation(
            seg, wld[:], AF.Sign),
            wait=[(s_aux, p.val(s_aux))], inc=s_ack)
        p.op("vector", lambda e, seg=seg: e.tensor_mul(
            seg, seg, wabs[:]),
            wait=[(s_ack, p.val(s_ack))], inc=s_t)
    tv_all = p.val(s_t)
    p.op("scalar", lambda e: e.nop(), wait=[(s_t, tv_all)], inc=s_ack)
    t_ack = p.val(s_ack)

    for tt in range(4):
        wh = [] if tt == 0 else [(s_pj, p.val(s_pj))]
        p.dma("sync", h_sb.ap().rearrange("p (k c) -> p k c", c=512),
              h_in.rearrange("(k p) c -> p k c", p=128)[:, :, tt * 512:(tt + 1) * 512],
              wait=wh, inc=s_hl)
        hv = p.val(s_hl)
        for oc in range(KD):
            bank = pj[oc % 2]
            w0 = [(s_ack, t_ack), (s_hl, hv)]
            if tt * KD + oc >= 2:
                w0.append((s_ev, p.val(s_ev) - 1))
            for k in range(NHC):
                p.op("tensor", lambda e, k=k, oc=oc, bank=bank:
                     e.matmul(bank[:],
                              wt_full[:, oc * NHC * 128 + k * 128:
                                      oc * NHC * 128 + (k + 1) * 128],
                              h_sb[:, k * 512:(k + 1) * 512],
                              start=(k == 0), stop=(k == NHC - 1)),
                     wait=(w0 if k == 0 else ()),
                     inc=(s_pj if k == NHC - 1 else None))
            p.op("vector", lambda e, oc=oc, bank=bank: e.tensor_scalar(
                stage[:, oc * 512:(oc + 1) * 512], bank[:],
                b2_sb[:, oc:oc + 1], None, ALU.add),
                wait=[(s_pj, p.val(s_pj))], inc=s_ev)
        evv = p.val(s_ev)
        for i4 in range(4):
            i = tt * 4 + i4
            w_tile = wt[i % 2]
            for k0 in (0, 4):
                bank2 = tpf[k0 // 4]
                wv2 = [(s_ev, evv)]
                if p.val(s_tr) >= 2:
                    wv2.append((s_tr, p.val(s_tr) - 1))
                if i >= 2 and k0 == 0:
                    wv2.append((s_sq1, i - 1))
                    wv2.append((s_st, 16 * (i - 1)))
                for j in range(4):
                    p.op("tensor", lambda e, i4=i4, j=j, k0=k0, bank2=bank2:
                         e.transpose(bank2[:, j * 128:(j + 1) * 128],
                                     stage[:, (k0 + j) * 512 + i4 * 128:
                                           (k0 + j) * 512 + (i4 + 1) * 128],
                                     identf_sb[:]),
                         wait=(wv2 if j == 0 else ()),
                         inc=(s_tb if j == 3 else None))
                tbv = p.val(s_tb)
                p.op("vector", lambda e, k0=k0, w_tile=w_tile, bank2=bank2:
                     e.tensor_copy(out=w_tile[:, k0 * 128:(k0 + 4) * 128],
                                   in_=bank2[:]),
                     wait=[(s_tb, tbv)], inc=s_tr)
            trv = p.val(s_tr)
            p.op("vector", lambda e, i=i, w_tile=w_tile: e.reduce_sum(
                ln.msum[:, i:i + 1], w_tile[:], AX.X), wait=[(s_tr, trv)],
                inc=s_add1)
            av = p.val(s_add1)
            p.dma("sync", o_f.rearrange("(t p) d -> p t d", p=128)[:, i, :],
                  w_tile[:], wait=[(s_add1, av)], inc=s_st)
            p.op("scalar", lambda e, i=i, w_tile=w_tile: e.activation(
                ln.sc[:], w_tile[:], AF.Square,
                accum_out=ln.sqs[:, i:i + 1]),
                wait=[(s_add1, av)], inc=s_sq1)
    p.op("vector", lambda e: e.engine_nop(), wait=[(s_sq1, NT)], inc=s_aux)
    p.op("scalar", lambda e: e.nop(), wait=[(s_aux, p.val(s_aux))], inc=s_ack)
    p.op("vector", lambda e: e.tensor_scalar_mul(
        ln.mean[:, 0:NT], ln.msum[:, 0:NT], inv_d),
        wait=[(s_ack, p.val(s_ack))], inc=s_aux)
    p.op("scalar", lambda e: e.nop(), wait=[(s_aux, p.val(s_aux))], inc=s_ack)
    p.op("vector", lambda e: e.tensor_mul(
        ln.m2[:, 0:NT], ln.mean[:, 0:NT], ln.mean[:, 0:NT]),
        wait=[(s_ack, p.val(s_ack))], inc=s_aux)
    p.op("scalar", lambda e: e.nop(), wait=[(s_aux, p.val(s_aux))], inc=s_ack)
    p.op("vector", lambda e: e.scalar_tensor_tensor(
        ln.vr[:, 0:NT], ln.sqs[:, 0:NT], inv_d, ln.m2[:, 0:NT],
        ALU.mult, ALU.subtract), wait=[(s_ack, p.val(s_ack))], inc=s_aux)
    p.op("scalar", lambda e: e.activation(
        ln.vln[:, 0:NT], ln.vr[:, 0:NT], AF.Ln, bias=ln.epsb[:]),
        wait=[(s_aux, p.val(s_aux))], inc=s_ack)
    p.op("vector", lambda e: e.engine_nop(), wait=[(s_ack, p.val(s_ack))],
         inc=s_aux)
    p.op("scalar", lambda e: e.activation(
        ln.rstd[:, 0:NT], ln.vln[:, 0:NT], AF.Exp, scale=-0.5),
        wait=[(s_aux, p.val(s_aux))], inc=s_ack)
    rstd_ack = p.val(s_ack)
    for i in range(NT):
        xb = xs[i % 2]
        w_tile = wt[i % 2]
        wy = [(s_st, 16 * NT), (s_add1, NT)] if i < 2 else [(s_g, p.val(s_g))]
        p.dma("sync", xb[:], o_f.rearrange("(t p) d -> p t d", p=128)[:, i, :],
              wait=wy, inc=s_y)
        p.op("vector", lambda e, i=i, xb=xb, w_tile=w_tile: e.tensor_scalar(
            w_tile[:], xb[:], ln.mean[:, i:i + 1], ln.rstd[:, i:i + 1],
            ALU.subtract, ALU.mult),
            wait=[(s_y, p.val(s_y)), (s_ack, rstd_ack)], inc=s_g)
        p.dma("sync", xb[:], x3_in.rearrange("(t p) d -> p t d", p=128)[:, i, :],
              wait=[(s_g, p.val(s_g))], inc=s_y)
        p.op("vector", lambda e, w_tile=w_tile, xb=xb: e.tensor_add(
            w_tile[:], w_tile[:], xb[:]), wait=[(s_y, p.val(s_y))], inc=s_g)
        p.dma("sync", o_x4.rearrange("(t p) d -> p t d", p=128)[:, i, :],
              w_tile[:], wait=[(s_g, p.val(s_g))], inc=s_st)
    return p


def host_inputs_C1(resB, full_x, Wo, bo):
    xf = full_x.reshape(B * S, D)
    maps = []
    for c in range(NCORES):
        b = c // 4
        cols = slice((c % 4) * RPC, (c % 4 + 1) * RPC)
        ctx = np.concatenate([np.asarray(resB[4 * b + hg]["ctx"])[:, cols]
                              for hg in range(4)], axis=0)
        maps.append({"ctx": ctx, "x": f32(xf[c * RPC:(c + 1) * RPC]),
                     "woT": f32(Wo.T), "bo": f32(bo).reshape(1, D),
                     "identf": _IDENT})
    return maps


def host_inputs_C2(resC1, w_qkv, ln1_w, ln1_b):
    maps = []
    for c in range(NCORES):
        maps.append({"x1": np.asarray(resC1[c]["x1"]),
                     "wqkv": f32(w_qkv),
                     "ln1w": f32(ln1_w).reshape(1, D),
                     "ln1b": f32(ln1_b).reshape(1, D),
                     "identf": _IDENT})
    return maps


def _chunked(v):
    return f32(np.asarray(v).reshape(-1, 128).T)


_PROGS = {}


def _prog(name, builder):
    if name not in _PROGS:
        _PROGS[name] = builder().build()
    return _PROGS[name]


def _run(name, builder, maps):
    nc = _prog(name, builder)
    return run_bass_kernel_spmd(nc, maps, core_ids=list(range(NCORES))).results


def kernel(L0_attn_hidden_states, L0_attn_Wq, L0_attn_Wk, L0_attn_Wv,
           L0_attn_Wo, L0_attn_bo, L0_ffn_ln1_w, L0_ffn_ln1_b, L0_ffn_w_qkv,
           L0_ffn_w_proj, L0_ffn_ln2_w, L0_ffn_ln2_b, L0_ffn_gate_w,
           L0_ffn_gate_b, L0_ffn_w1, L0_ffn_w2, L1_attn_ln1_g, L1_attn_ln1_b,
           L1_attn_Wk, L1_attn_Wv, L1_attn_Wr, L1_attn_Wo, L1_attn_time_decay,
           L1_attn_time_first, L1_attn_mix_k, L1_attn_mix_v, L1_attn_mix_r,
           L1_attn_ln2_g, L1_attn_ln2_b, L1_attn_Wfk, L1_attn_Wfv,
           L1_attn_Wfr, L1_ffn_W1, L1_ffn_b1, L1_ffn_W2, L1_ffn_b2):
    x = np.asarray(L0_attn_hidden_states, dtype=np.float32)
    IDB = _IDENT.astype(BF)

    resA = _run("A", build_launch_A,
                host_inputs_A(x, f32(L0_attn_Wq), f32(L0_attn_Wk),
                              f32(L0_attn_Wv)))
    resB = _run("B", build_launch_B, host_inputs_B(resA))
    resC1 = _run("C1", build_launch_C1,
                 host_inputs_C1(resB, x, f32(L0_attn_Wo), f32(L0_attn_bo)))
    resC2 = _run("C2", build_launch_C2,
                 host_inputs_C2(resC1, f32(L0_ffn_w_qkv), f32(L0_ffn_ln1_w),
                                f32(L0_ffn_ln1_b)))
    resD = _run("D", build_launch_D, host_inputs_D(resC2))

    mapsE1 = []
    for c in range(NCORES):
        b = c // 4
        cols = slice((c % 4) * RPC, (c % 4 + 1) * RPC)
        ctx = np.concatenate([np.asarray(resD[4 * b + hg]["ctx"])[:, cols]
                              for hg in range(4)], axis=0)
        mapsE1.append({"ctx": ctx, "y0": np.asarray(resC2[c]["y0"]),
                       "wproj": f32(L0_ffn_w_proj),
                       "ln2w": f32(L0_ffn_ln2_w).reshape(1, D),
                       "ln2b": f32(L0_ffn_ln2_b).reshape(1, D),
                       "gatew": f32(L0_ffn_gate_w),
                       "gateb": f32(L0_ffn_gate_b).reshape(1, 8),
                       "identf": _IDENT})
    resE1 = _run("E1", build_launch_E1, mapsE1)
    mapsE2 = [{"z_fm": np.asarray(resE1[c]["z_fm"]),
               "mask": np.asarray(resE1[c]["mask"]),
               "w1": f32(L0_ffn_w1), "w2": f32(L0_ffn_w2)}
              for c in range(NCORES)]
    resE2 = _run("E2", build_launch_E2, mapsE2)
    mapsE3 = [{"moe_fm": np.asarray(resE2[c]["moe_fm"]),
               "y1": np.asarray(resE1[c]["y1"]),
               "x1": np.asarray(resC1[c]["x1"]), "identb": IDB}
              for c in range(NCORES)]
    resE3 = _run("E3", build_launch_E3, mapsE3)

    x2 = [np.asarray(resE3[c]["x2"]) for c in range(NCORES)]
    mapsF1 = []
    for c in range(NCORES):
        if c % 4 == 0:
            pad = np.zeros((OV, D), np.float32)
        else:
            pad = x2[c - 1][RPC - OV:]
        mapsF1.append({"x2pad": f32(np.concatenate([pad, x2[c]], 0)),
                       "ln1g": f32(L1_attn_ln1_g).reshape(1, D),
                       "ln1b": f32(L1_attn_ln1_b).reshape(1, D),
                       "identf": _IDENT})
    resF1 = _run("F1", build_launch_F1, mapsF1)

    mixes = np.concatenate([_chunked(L1_attn_mix_k), _chunked(L1_attn_mix_v),
                            _chunked(L1_attn_mix_r)], axis=1)
    mapsF2 = []
    for c in range(NCORES):
        m0 = np.zeros((128, 1), np.float32) if c % 4 == 0 else             np.ones((128, 1), np.float32)
        mapsF2.append({"xn_fm": np.asarray(resF1[c]["xn_fm"]),
                       "Wk": f32(L1_attn_Wk), "Wv": f32(L1_attn_Wv),
                       "Wr": f32(L1_attn_Wr), "mixes": mixes, "m0vec": m0})
    resF2 = _run("F2", build_launch_F2, mapsF2)

    tdc = _chunked(L1_attn_time_decay)
    tfc = _chunked(L1_attn_time_first)
    mapsF3 = []
    for c in range(NCORES):
        tf = tfc if c % 4 == 0 else np.zeros_like(tfc)
        mapsF3.append({"k_fm": np.asarray(resF2[c]["k_fm"]),
                       "v_fm": np.asarray(resF2[c]["v_fm"]),
                       "r_fm": np.asarray(resF2[c]["r_fm"]),
                       "tdvec": tdc, "tfvec": tf})
    resF3 = _run("F3", build_launch_F3, mapsF3)

    mapsF4 = [{"rw_fm": np.asarray(resF3[c]["rw_fm"]),
               "y0b": np.asarray(resF1[c]["y0b"]),
               "Wo": f32(L1_attn_Wo),
               "ln2g": f32(L1_attn_ln2_g).reshape(1, D),
               "ln2b": f32(L1_attn_ln2_b).reshape(1, D),
               "identf": _IDENT} for c in range(NCORES)]
    resF4 = _run("F4", build_launch_F4, mapsF4)

    mapsF5 = [{"yn_fm": np.asarray(resF4[c]["yn_fm"]),
               "y1i": np.asarray(resF4[c]["y1i"]),
               "x2": x2[c], "Wfk": f32(L1_attn_Wfk), "Wfv": f32(L1_attn_Wfv),
               "Wfr": f32(L1_attn_Wfr), "identf": _IDENT}
              for c in range(NCORES)]
    resF5 = _run("F5", build_launch_F5, mapsF5)

    mapsF6a = [{"x3": np.asarray(resF5[c]["x3"]),
                "W1T": f32(np.asarray(L1_ffn_W1).T),
                "b1": f32(L1_ffn_b1).reshape(1, 4096),
                "identf": _IDENT} for c in range(NCORES)]
    resF6a = _run("F6a", build_launch_F6a, mapsF6a)

    mapsF6b = [{"h_fm": np.asarray(resF6a[c]["h_fm"]),
                "W2T": f32(np.asarray(L1_ffn_W2).T),
                "b2": f32(L1_ffn_b2).reshape(1, D),
                "x3": np.asarray(resF5[c]["x3"]),
                "identf": _IDENT} for c in range(NCORES)]
    resF6b = _run("F6b", build_launch_F6b, mapsF6b)

    out = np.concatenate([np.asarray(resF6b[c]["x4"]) for c in range(NCORES)],
                         axis=0)
    return out.reshape(B, S, D).astype(np.float32)
